# version 18
# baseline (speedup 1.0000x reference)
import numpy as np
import ml_dtypes
from contextlib import ExitStack

import concourse.bass as bass
import concourse.mybir as mybir
from concourse.bass_utils import run_bass_kernel_spmd

F32 = mybir.dt.float32
BF16 = mybir.dt.bfloat16
AF = mybir.ActivationFunctionType
ALU = mybir.AluOpType
AX = mybir.AxisListType

T = 4096
D = 2048
DL = 512
DC = 256
E = 8
I = 1408
SI = 2816
NT = T // 128
LN_EPS = 1e-5
NEG = -1e30

ENGINES = ("PE", "ACT", "DVE", "POOL", "SP")
SECTION = {"PE": "tensor", "ACT": "scalar", "DVE": "vector", "POOL": "gpsimd", "SP": "sync"}


NLANES = 32
QLANES = {"SP": 32, "POOL": 32, "ACT": 8}


class Op:
    __slots__ = ("engine", "seq", "emit", "inc", "is_dma", "needed", "cum",
                 "waits", "deps", "sem")

    def __init__(self, engine, seq, emit, is_dma):
        self.engine = engine
        self.seq = seq
        self.emit = emit
        self.is_dma = is_dma
        self.inc = 16 if is_dma else 1
        self.needed = is_dma
        self.cum = None
        self.sem = None
        self.waits = []
        self.deps = []


class Prog:

    def __init__(self, nc, block, sems):
        self.nc = nc
        self.block = block
        self.sems = sems
        self.ops = []
        self.emitted = 0
        self.last_writer = {}
        self.readers = {}
        self.totals = {s: 0 for s in sems}
        self.dma_count = {"SP": 0, "POOL": 0, "ACT": 0}
        self.barrier_vals = {}
        self.clock = {e: {} for e in ENGINES}

    def record(self, engine, emit, reads=(), writes=(), is_dma=False):
        op = Op(engine, len(self.ops), emit, is_dma)
        if is_dma:
            n = self.dma_count[engine]
            self.dma_count[engine] = n + 1
            nl = QLANES[engine]
            op.sem = f"{engine}dma{n % nl}"
            op.cum = 16 * (n // nl + 1)
        else:
            op.sem = engine
        deps = {}
        for k in reads:
            w = self.last_writer.get(k)
            if w is not None:
                deps[w.seq] = w
        for k in writes:
            w = self.last_writer.get(k)
            if w is not None:
                deps[w.seq] = w
            for r in self.readers.get(k, ()):
                deps[r.seq] = r
        op.deps = [d for s, d in sorted(deps.items()) if d is not op]
        self.ops.append(op)
        for k in reads:
            self.readers.setdefault(k, []).append(op)
        for k in writes:
            self.last_writer[k] = op
            self.readers[k] = []
        return op

    def flush(self):
        batch = self.ops[self.emitted:]
        if not batch:
            return
        start_seq = self.emitted
        for op in batch:
            for d in op.deps:
                if d.seq >= start_seq:
                    d.needed = True
        for eng in ENGINES:
            for op in reversed(batch):
                if op.engine == eng and not op.is_dma:
                    op.needed = True
                    break
        for op in self.last_writer.values():
            if op.seq >= start_seq:
                op.needed = True
        for lst in self.readers.values():
            for op in lst:
                if op.seq >= start_seq:
                    op.needed = True
        for eng in ENGINES:
            cum = self.totals.get(eng, 0)
            for op in batch:
                if op.engine != eng or op.is_dma:
                    continue
                if op.needed:
                    cum += op.inc
                op.cum = cum
            if eng in self.totals:
                self.totals[eng] = cum
        for op in batch:
            if op.is_dma:
                self.totals[op.sem] = max(self.totals[op.sem], op.cum)
        bar = dict(self.barrier_vals)
        all_bar_waits = {}
        for eng_name in ENGINES:
            bar_waits = []
            clk = self.clock[eng_name]
            for sname, v in sorted(bar.items()):
                if sname == eng_name or clk.get(sname, 0) >= v:
                    continue
                clk[sname] = v
                bar_waits.append((sname, v))
            all_bar_waits[eng_name] = bar_waits
        for op in batch:
            ws = {}
            if op.is_dma and op.cum > 16:
                ws[op.sem] = op.cum - 16
            for d in op.deps:
                if d.seq < start_seq:
                    continue
                if d.engine == op.engine == "PE":
                    continue
                assert d.needed and d.cum is not None
                if d.cum > ws.get(d.sem, 0):
                    ws[d.sem] = d.cum
            clk = self.clock[op.engine]
            for sname, v in sorted(ws.items()):
                if clk.get(sname, 0) >= v:
                    continue
                clk[sname] = v
                op.waits.append((sname, v))
        for eng_name in ENGINES:
            eng_ops = [o for o in batch if o.engine == eng_name]
            bar_waits = all_bar_waits[eng_name]
            if not eng_ops and not bar_waits:
                continue
            self._emit_section(eng_name, bar_waits, eng_ops)
        self.barrier_vals = {s: v for s, v in self.totals.items() if v > 0}
        self.emitted = len(self.ops)

    def _emit_section(self, eng_name, bar_waits, eng_ops):
        sems = self.sems

        def section(eng, _bar=bar_waits, _ops=eng_ops):
            for sname, v in _bar:
                eng.wait_ge(sems[sname], v)
            for op in _ops:
                for sname, v in op.waits:
                    eng.wait_ge(sems[sname], v)
                inst = op.emit(eng)
                if op.needed:
                    inst.then_inc(sems[op.sem], op.inc)

        getattr(self.block, SECTION[eng_name])(section)

    def finish(self):
        self.flush()
        sems = self.sems
        totals = {s: v for s, v in self.totals.items() if v > 0}

        def section(eng):
            for sname, v in sorted(totals.items()):
                eng.wait_ge(sems[sname], v)

        self.block.sync(section)


def build_program(cfg=None):
    cfg = dict(cfg or {})
    n_tiles = cfg.get("n_tiles", NT)
    moe_experts = cfg.get("moe_experts", list(range(E)))
    shared_chunks = cfg.get("shared_chunks", [(0, 6), (6, 6), (12, 6), (18, 4)])
    run_phase_b = cfg.get("phase_b", True)
    n_groups = cfg.get("n_groups", 8)
    sparse = cfg.get("sparse", True)
    NST = 24

    nc = bass.Bass("TRN2", target_bir_lowering=False)

    def din(name, shape, dt=F32):
        return nc.dram_tensor(name, list(shape), dt, kind="ExternalInput").ap()

    x_h = din("x", (T, D))
    wtkvT_h = din("wtkvT", (D, DL))
    wtkv_h = din("wtkv", (DL, D))
    worT_h = din("worT", (DL, D))
    wcompT_h = din("wcompT", (DL, DC))
    wqT_h = din("wqT", (DC, DC))
    wkvT_h = din("wkvT", (DC, 2 * DC))
    wc2lT_h = din("wc2lT", (DC, DL))
    gamma_h = din("gamma", (1, DC))
    beta_h = din("beta", (1, DC))
    bout_h = din("bout", (1, D))
    sdownb_h = din("sdownb", (1, D))
    routerT_h = din("routerT", (D, E))
    gate_h = din("gate_w", (E, D, I))
    up_h = din("up_w", (E, D, I))
    down_h = din("down_w", (E, I, D))
    sgate_h = din("sgate_w", (D, SI))
    sup_h = din("sup_w", (D, SI))
    sdown_h = din("sdown_w", (SI, D))
    sgb_h = din("sgb2", (128, 22))
    sub_h = din("sub2", (128, 22))
    rope_h = din("ropetab", (T, 4 * DC))
    ident_h = din("ident", (128, 128), BF16)
    mask_h = din("mask", (128, 128))
    ones_h = din("ones1", (1, 128))

    ln2h = nc.dram_tensor("ln2h", [16, 128, T], BF16, kind="Internal").ap()
    ln2tok = nc.dram_tensor("ln2tok", [T, D], BF16, kind="Internal").ap()
    sorted_h = nc.dram_tensor("sorted_h", [24 * 512, D], BF16, kind="Internal").ap()
    ysAB = nc.dram_tensor("ysAB", [24 * 512, 2 * D], F32, kind="Internal").ap()
    zrows = nc.dram_tensor("zrows", [512, D], BF16, kind="Internal").ap()
    e_dram = nc.dram_tensor("e_dram", [1, 24], F32, kind="Internal").ap()
    gupA_bf = nc.dram_tensor("gupA_bf", [E * D, 1536], BF16, kind="Internal").ap()
    gupB_bf = nc.dram_tensor("gupB_bf", [E * D, 1280], BF16, kind="Internal").ap()
    downc_bf = nc.dram_tensor("downc_bf", [E * I, D], BF16, kind="Internal").ap()
    sgate_bf = nc.dram_tensor("sgate_bf", [D, SI], BF16, kind="Internal").ap()
    sup_bf = nc.dram_tensor("sup_bf", [D, SI], BF16, kind="Internal").ap()
    sdown_bf = nc.dram_tensor("sdown_bf", [SI, D], BF16, kind="Internal").ap()
    gupA_h = nc.dram_tensor("gupA", [E * D, 1536], F32, kind="ExternalInput").ap()
    gupB_h = nc.dram_tensor("gupB", [E * D, 1280], F32, kind="ExternalInput").ap()
    cumtri_h = nc.dram_tensor("cumtri", [128, 128], BF16, kind="ExternalInput").ap()
    ones128b_h = nc.dram_tensor("ones128b", [128, 1], BF16, kind="ExternalInput").ap()
    starts24_h = nc.dram_tensor("starts24", [128, 1], F32, kind="ExternalInput").ap()
    kp16_h = nc.dram_tensor("kp16", [128, 16], F32, kind="ExternalInput").ap()
    kp11_h = nc.dram_tensor("kp11", [128, 11], F32, kind="ExternalInput").ap()
    out_h = nc.dram_tensor("out", [T, D], F32, kind="ExternalOutput").ap()

    with ExitStack() as outer:
        sem_names = list(ENGINES) + [f"{q}dma{j}" for q in ("SP", "POOL", "ACT")
                                     for j in range(QLANES[q])]
        sems = {s: outer.enter_context(nc.semaphore("sem_" + s)) for s in sem_names}
        ce_sb = outer.enter_context(nc.sbuf_tensor("ce_sb", [128, NT * E], F32))
        ce12_sb = outer.enter_context(nc.sbuf_tensor("ce12_sb", [128, 2 * NT], F32))
        gr12_sb = outer.enter_context(nc.sbuf_tensor("gr12_sb", [128, 2 * NT], F32))
        eq1_sb = outer.enter_context(nc.sbuf_tensor("eq1_sb", [128, NT * E], F32))
        eq2_sb = outer.enter_context(nc.sbuf_tensor("eq2_sb", [128, NT * E], F32))
        dst_sb = outer.enter_context(nc.sbuf_tensor("dst_sb", [128, 2 * NT], mybir.dt.int32))
        block = outer.enter_context(nc.Block())
        prog = Prog(nc, block, sems)

        with ExitStack() as pa:
            def sbuf(name, shape, dt):
                return pa.enter_context(nc.sbuf_tensor(name, shape, dt))

            def psum(name, shape, dt):
                return pa.enter_context(nc.psum_tensor(name, shape, dt))

            wcomb_bf = sbuf("wcomb_bf", [128, 4 * D], BF16)
            wtkvT_bf = sbuf("wtkvT_bf", [128, 16 * DL], BF16)
            wcompT_bf = sbuf("wcompT_bf", [128, 4 * DC], BF16)
            wqT_bf = sbuf("wqT_bf", [128, 2 * DC], BF16)
            wkvT_bf = sbuf("wkvT_bf", [128, 2 * 2 * DC], BF16)
            wc2lT_bf = sbuf("wc2lT_bf", [128, 2 * DL], BF16)
            routerT_bf = sbuf("routerT_bf", [128, 16 * E], BF16)
            ident_sb = sbuf("ident_sb", [128, 128], BF16)
            mask_sb = sbuf("mask_sb", [128, 128], F32)
            gamma_bc = sbuf("gamma_bc", [128, DC], F32)
            beta_bc = sbuf("beta_bc", [128, DC], F32)
            bout_bc = sbuf("bout_bc", [128, D], F32)
            sdownb_bc = sbuf("sdownb_bc", [128, D], F32)
            ones_sb = sbuf("ones_sb", [1, 128], F32)
            pt_a = psum("pt_a", [128, 1024], BF16)
            pt_b = psum("pt_b", [128, 1024], BF16)
            pmm1 = psum("pmm1", [128, 512], F32)
            pmm2 = psum("pmm2", [128, 512], F32)
            pq = psum("pq", [128, 512], F32)
            pout2 = psum("pout2", [128, 1024], F32)
            p_o = psum("p_o", [128, 512], F32)
            cumtri_sb = sbuf("cumtri_sb", [128, 128], BF16)
            ones128b_sb = sbuf("ones128b_sb", [128, 1], BF16)
            starts24_sb = sbuf("starts24_sb", [128, 1], F32)
            sel8_bf = sbuf("sel8_bf", [128, E], BF16)
            offs_sb = sbuf("offs_sb", [1, E], F32)
            trow = sbuf("trow", [1, 32], F32)
            trow_i = sbuf("trow_i", [1, E], mybir.dt.int32)
            et8 = sbuf("et8", [128, E], F32)
            e_colv = sbuf("e_colv", [128, 1], F32)
            base_bc_sb = sbuf("base_bc_sb", [128, E], F32)
            dcol = sbuf("dcol", [128, 4], F32)

            C_SUM, C_MEAN, C_MSQ, C_VAR, C_LNV, C_RSTD, C_NEGMAX, C_SUME, C_RECIP, \
                C_M1, C_M2, C_DEN, C_RDEN, C_NM8, C_S8, C_R8 = range(16)

            def col(j):
                return cols[:, j:j + 1]

            prog.record("SP", lambda eng: eng.dma_start(out=ident_sb[:], in_=ident_h[:]),
                        writes=[("ident",)], is_dma=True)
            prog.record("SP", lambda eng: eng.dma_start(out=mask_sb[:], in_=mask_h[:]),
                        writes=[("mask",)], is_dma=True)
            prog.record("SP", lambda eng: eng.dma_start(out=ones_sb[:], in_=ones_h[:]),
                        writes=[("ones",)], is_dma=True)
            prog.record("SP", lambda eng: eng.dma_start(out=cumtri_sb[:], in_=cumtri_h[:]),
                        writes=[("cumtri",)], is_dma=True)
            prog.record("SP", lambda eng: eng.dma_start(out=ones128b_sb[:], in_=ones128b_h[:]),
                        writes=[("ones128b",)], is_dma=True)
            prog.record("SP", lambda eng: eng.dma_start(out=starts24_sb[:], in_=starts24_h[:]),
                        writes=[("starts24",)], is_dma=True)
            prog.record("DVE", lambda eng: eng.memset(offs_sb[:], 0.0),
                        writes=[("offs",)])

            def cast_load(dst_ap, src_ap, key):
                prog.record("POOL", lambda eng: eng.dma_start(out=dst_ap, in_=src_ap),
                            writes=[key], is_dma=True)

            cast_load(wtkvT_bf[:].rearrange("p (k n) -> p k n", k=16),
                      wtkvT_h.rearrange("(k p) n -> p k n", p=128), ("wtkvT",))
            cast_load(wcompT_bf[:].rearrange("p (k n) -> p k n", k=4),
                      wcompT_h.rearrange("(k p) n -> p k n", p=128), ("wcompT",))
            cast_load(wqT_bf[:].rearrange("p (k n) -> p k n", k=2),
                      wqT_h.rearrange("(k p) n -> p k n", p=128), ("wqT",))
            cast_load(wkvT_bf[:].rearrange("p (k n) -> p k n", k=2),
                      wkvT_h.rearrange("(k p) n -> p k n", p=128), ("wkvT",))
            cast_load(wc2lT_bf[:].rearrange("p (k n) -> p k n", k=2),
                      wc2lT_h.rearrange("(k p) n -> p k n", p=128), ("wc2lT",))
            cast_load(routerT_bf[:].rearrange("p (k n) -> p k n", k=16),
                      routerT_h.rearrange("(k p) n -> p k n", p=128), ("routerT",))

            with ExitStack() as prep:
                wtkv_f = prep.enter_context(nc.sbuf_tensor("wtkv_f", [128, 4 * D], F32))
                worT_f = prep.enter_context(nc.sbuf_tensor("worT_f", [128, 4 * D], F32))
                prog.record("SP", lambda eng: eng.dma_start(
                    out=wtkv_f[:].rearrange("p (k n) -> p k n", k=4),
                    in_=wtkv_h.rearrange("(k p) n -> p k n", p=128)),
                    writes=[("wtkv_f",)], is_dma=True)
                prog.record("SP", lambda eng: eng.dma_start(
                    out=worT_f[:].rearrange("p (k n) -> p k n", k=4),
                    in_=worT_h.rearrange("(k p) n -> p k n", p=128)),
                    writes=[("worT_f",)], is_dma=True)
                prog.record("DVE", lambda eng: eng.tensor_tensor(
                    out=wcomb_bf[:], in0=wtkv_f[:], in1=worT_f[:], op=ALU.add),
                    reads=[("wtkv_f",), ("worT_f",)], writes=[("wcomb",)])

                with ExitStack() as prep2:
                    rows = prep2.enter_context(nc.sbuf_tensor("rows", [1, 2 * D + 2 * DC], F32))
                    prog.record("SP", lambda eng: eng.dma_start(out=rows[0:1, 0:D], in_=bout_h[:]),
                                writes=[("rows",)], is_dma=True)
                    prog.record("SP", lambda eng: eng.dma_start(out=rows[0:1, D:2 * D], in_=sdownb_h[:]),
                                writes=[("rows",)], is_dma=True)
                    prog.record("SP", lambda eng: eng.dma_start(out=rows[0:1, 2 * D:2 * D + DC], in_=gamma_h[:]),
                                writes=[("rows",)], is_dma=True)
                    prog.record("SP", lambda eng: eng.dma_start(out=rows[0:1, 2 * D + DC:2 * D + 2 * DC], in_=beta_h[:]),
                                writes=[("rows",)], is_dma=True)

                    def bc_row(dst, src_off, width, tag):
                        nchunk = (width + 511) // 512
                        for s in range(nchunk):
                            w = min(512, width - s * 512)
                            prog.record("PE", lambda eng, s=s, w=w: eng.matmul(
                                pout2[:, s % 2 * 512:s % 2 * 512 + w],
                                ones_sb[0:1, :],
                                rows[0:1, src_off + s * 512:src_off + s * 512 + w],
                                start=True, stop=True),
                                reads=[("ones",), ("rows",)], writes=[("pbc", s % 2)])
                            prog.record("DVE", lambda eng, s=s, w=w: eng.tensor_copy(
                                out=dst[:, s * 512:s * 512 + w],
                                in_=pout2[:, s % 2 * 512:s % 2 * 512 + w]),
                                reads=[("pbc", s % 2)], writes=[(tag,)])

                    bc_row(bout_bc, 0, D, "bout_bc")
                    bc_row(sdownb_bc, D, D, "sdownb_bc")
                    bc_row(gamma_bc, 2 * D, DC, "gamma_bc")
                    bc_row(beta_bc, 2 * D + DC, DC, "beta_bc")
                    prog.flush()
            prog.flush()

            if sparse:
                prog.record("DVE", lambda eng: eng.memset(exp_bf[:], 0.0),
                            writes=[("exp", "src")])
                for zq in range(2):
                    prog.record("POOL", lambda eng, zq=zq: eng.dma_start(
                        out=zrows[zq * 256:(zq + 1) * 256, :].rearrange("(q p) c -> p q c", p=128),
                        in_=exp_bf[:].rearrange("p (q c) -> p q c", q=2)),
                        reads=[("exp", "src")], writes=[("zrows",)], is_dma=True)
                for sp_i in range(24):
                    prog.record("POOL", lambda eng, sp_i=sp_i: eng.dma_start(
                        out=sorted_h[sp_i * 512:(sp_i + 1) * 512, :], in_=zrows[:]),
                        reads=[("zrows",)], writes=[("sorted",)], is_dma=True)
                down_srcv = down_h.rearrange("e i d -> (e i) d")
                for ch in range(8):
                    r0, r1 = ch * 2048, (ch + 1) * 2048
                    prog.record("POOL", lambda eng, r0=r0, r1=r1: eng.dma_start(
                        out=gupA_bf[r0:r1, :], in_=gupA_h[r0:r1, :]),
                        writes=[("wbf",)], is_dma=True)
                    prog.record("POOL", lambda eng, r0=r0, r1=r1: eng.dma_start(
                        out=gupB_bf[r0:r1, :], in_=gupB_h[r0:r1, :]),
                        writes=[("wbf",)], is_dma=True)
                    d0, d1 = ch * 1408, (ch + 1) * 1408
                    prog.record("POOL", lambda eng, d0=d0, d1=d1, down_srcv=down_srcv: eng.dma_start(
                        out=downc_bf[d0:d1, :], in_=down_srcv[d0:d1, :]),
                        writes=[("wbf",)], is_dma=True)
                    s0, s1 = ch * 256, (ch + 1) * 256
                    prog.record("POOL", lambda eng, s0=s0, s1=s1: eng.dma_start(
                        out=sgate_bf[s0:s1, :], in_=sgate_h[s0:s1, :]),
                        writes=[("wbf",)], is_dma=True)
                    prog.record("POOL", lambda eng, s0=s0, s1=s1: eng.dma_start(
                        out=sup_bf[s0:s1, :], in_=sup_h[s0:s1, :]),
                        writes=[("wbf",)], is_dma=True)
                    z0, z1 = ch * 352, (ch + 1) * 352
                    prog.record("POOL", lambda eng, z0=z0, z1=z1: eng.dma_start(
                        out=sdown_bf[z0:z1, :], in_=sdown_h[z0:z1, :]),
                        writes=[("wbf",)], is_dma=True)

            kT0 = sbuf("kT0", [128, T], BF16)
            kT1 = sbuf("kT1", [128, T], BF16)
            v_sb = sbuf("v_sb", [128, NT * DC], BF16)
            stage = sbuf("stage", [128, 16 * 512], BF16)
            x_sb = [sbuf(f"x_sb{p}", [128, D], F32) for p in range(2)]
            rope_sb = [sbuf(f"rope_sb{p}", [128, 4 * DC], F32) for p in range(2)]
            xn_bf = sbuf("xn_bf", [128, D], BF16)
            xnT = sbuf("xnT", [128, D], BF16)
            kvlat_bf = sbuf("kvlat_bf", [128, DL], BF16)
            kvlatT = sbuf("kvlatT", [128, DL], BF16)
            c_sb = sbuf("c_sb", [128, DC], F32)
            t1_sb = sbuf("t1_sb", [128, DC], F32)
            t2_sb = sbuf("t2_sb", [128, DC], F32)
            ch_bf = sbuf("ch_bf", [128, DC], BF16)
            chT = sbuf("chT", [128, DC], BF16)
            rtmp = c_sb
            rcos = t1_sb
            qr_bf = sbuf("qr_bf", [128, DC], BF16)
            kr_bf = sbuf("kr_bf", [128, DC], BF16)
            qrTb = [sbuf(f"qrT{p}", [128, DC], BF16) for p in range(2)]
            scores = sbuf("scores", [128, T], F32)
            exp_bf = sbuf("exp_bf", [128, T], BF16)
            expT = sbuf("expT", [128, T], BF16)
            o_bf = sbuf("o_bf", [128, DC], BF16)
            oT = sbuf("oT", [128, DC], BF16)
            z_bf = sbuf("z_bf", [128, DL], BF16)
            zT = sbuf("zT", [128, DL], BF16)
            xb_sb = sbuf("xb_sb", [128, D], F32)
            r1_sb = sbuf("r1_sb", [128, D], F32)
            ln2_bf = sbuf("ln2_bf", [128, D], BF16)
            cols = sbuf("cols", [128, 16], F32)
            colsB_sb = sbuf("colsB_sb", [128, 16], F32)
            r8a = sbuf("r8a", [128, E], F32)
            r8b = sbuf("r8b", [128, E], F32)
            r8c = sbuf("r8c", [128, E], F32)

            def transpose_burst(rec, src_bf, n_blocks, dst, dst_off_blocks, tag, pt, pkey,
                                dkey=None):
                done = 0
                dk = dkey if dkey is not None else (tag, "dst")
                while done < n_blocks:
                    burst = min(8, n_blocks - done)
                    for b in range(burst):
                        rec("PE", lambda eng, b=b, done=done: eng.transpose(
                            pt[:, b * 128:(b + 1) * 128],
                            src_bf[:, (done + b) * 128:(done + b + 1) * 128],
                            ident_sb[:]),
                            reads=[(tag, "src"), ("ident",)], writes=[pkey])
                    o0 = (dst_off_blocks + done) * 128
                    rec("DVE", lambda eng, burst=burst, o0=o0: eng.tensor_copy(
                        out=dst[:, o0:o0 + burst * 128], in_=pt[:, 0:burst * 128]),
                        reads=[pkey], writes=[dk])
                    done += burst

            def ln_cols(rec, src_key, src_ap, width, dump_ap, dump_key, C, ck, c2k):
                rec("DVE", lambda eng: eng.reduce_sum(
                    out=C(C_SUM), in_=src_ap, axis=AX.X),
                    reads=[src_key], writes=[ck])
                rec("ACT", lambda eng: eng.activation(
                    out=dump_ap, in_=src_ap, func=AF.Square, bias=0.0, scale=1.0,
                    accum_out=C(C_MSQ)),
                    reads=[src_key], writes=[dump_key, c2k])
                rec("DVE", lambda eng: eng.tensor_scalar(
                    out=C(C_MEAN), in0=C(C_SUM), scalar1=1.0 / width, scalar2=None,
                    op0=ALU.mult), reads=[ck], writes=[ck])
                rec("DVE", lambda eng: eng.tensor_tensor(
                    out=C(C_VAR), in0=C(C_MEAN), in1=C(C_MEAN), op=ALU.mult),
                    reads=[ck], writes=[ck])
                rec("DVE", lambda eng: eng.tensor_scalar(
                    out=C(C_VAR), in0=C(C_MSQ), scalar1=1.0 / width,
                    scalar2=C(C_VAR), op0=ALU.mult, op1=ALU.subtract),
                    reads=[ck, c2k], writes=[ck])
                rec("DVE", lambda eng: eng.tensor_scalar(
                    out=C(C_VAR), in0=C(C_VAR), scalar1=LN_EPS, scalar2=None,
                    op0=ALU.add), reads=[ck], writes=[ck])
                rec("ACT", lambda eng: eng.activation(
                    out=C(C_LNV), in_=C(C_VAR), func=AF.Ln, bias=0.0, scale=1.0),
                    reads=[ck], writes=[ck])
                rec("ACT", lambda eng: eng.activation(
                    out=C(C_RSTD), in_=C(C_LNV), func=AF.Exp, bias=0.0, scale=-0.5),
                    reads=[ck], writes=[ck])

            def colB(j):
                return colsB_sb[:, j:j + 1]

            def issue_tile_loads(i, rec):
                par = i % 2
                rec("ACT", lambda eng, i=i, par=par: eng.dma_start(
                    out=x_sb[par][:], in_=x_h[i * 128:(i + 1) * 128, :]),
                    writes=[("x", par)], is_dma=True)
                rec("ACT", lambda eng, i=i, par=par: eng.dma_start(
                    out=rope_sb[par][:], in_=rope_h[i * 128:(i + 1) * 128, :]),
                    writes=[("rope", par)], is_dma=True)

            def front_tile(i, rec):
                par = i % 2
                issue_tile_loads(i, rec)
                x_t = x_sb[par]
                rope_t = rope_sb[par]
                xkey, rkey = ("x", par), ("rope", par)

                ln_cols(rec, xkey, x_t[:], D, xn_bf[:], ("xn", "src"), col, ("cols",), ("cols2",))
                rec("DVE", lambda eng, x_t=x_t: eng.tensor_scalar(
                    out=xn_bf[:], in0=x_t[:], scalar1=col(C_MEAN), scalar2=col(C_RSTD),
                    op0=ALU.subtract, op1=ALU.mult),
                    reads=[xkey, ("cols",)], writes=[("xn", "src")])

                transpose_burst(rec, xn_bf, 16, xnT, 0, "xn", pt_a, ("pt", 0))

                for k in range(16):
                    rec("PE", lambda eng, k=k: eng.matmul(
                        pmm1[:, 0:DL], xnT[:, k * 128:(k + 1) * 128],
                        wtkvT_bf[:, k * DL:(k + 1) * DL],
                        start=(k == 0), stop=(k == 15)),
                        reads=[("xn", "dst"), ("wtkvT",)], writes=[("pmm1",)])
                rec("ACT", lambda eng: eng.copy(out=kvlat_bf[:], in_=pmm1[:, 0:DL]),
                            reads=[("pmm1",)], writes=[("kvlat", "src")])
                transpose_burst(rec, kvlat_bf, 4, kvlatT, 0, "kvlat", pt_a, ("pt", 0))

                for k in range(4):
                    rec("PE", lambda eng, k=k: eng.matmul(
                        pq[:, 0:DC], kvlatT[:, k * 128:(k + 1) * 128],
                        wcompT_bf[:, k * DC:(k + 1) * DC],
                        start=(k == 0), stop=(k == 3)),
                        reads=[("kvlat", "dst"), ("wcompT",)], writes=[("pq",)])
                rec("DVE", lambda eng: eng.tensor_copy(out=c_sb[:], in_=pq[:, 0:DC]),
                            reads=[("pq",)], writes=[("c_sb",)])

                ln_cols(rec, ("c_sb",), c_sb[:], DC, ch_bf[:], ("ch", "src"), col, ("cols",), ("cols2",))
                rec("DVE", lambda eng: eng.tensor_scalar(
                    out=t1_sb[:], in0=c_sb[:], scalar1=col(C_MEAN), scalar2=col(C_RSTD),
                    op0=ALU.subtract, op1=ALU.mult),
                    reads=[("c_sb",), ("cols",)], writes=[("t1",)])
                rec("DVE", lambda eng: eng.tensor_tensor(
                    out=t2_sb[:], in0=t1_sb[:], in1=gamma_bc[:], op=ALU.mult),
                    reads=[("t1",), ("gamma_bc",)], writes=[("t2",)])
                rec("DVE", lambda eng: eng.tensor_tensor(
                    out=ch_bf[:], in0=t2_sb[:], in1=beta_bc[:], op=ALU.add),
                    reads=[("t2",), ("beta_bc",)], writes=[("ch", "src")])
                transpose_burst(rec, ch_bf, 2, chT, 0, "ch", pt_a, ("pt", 0))

                for k in range(2):
                    rec("PE", lambda eng, k=k: eng.matmul(
                        pq[:, 0:DC], chT[:, k * 128:(k + 1) * 128],
                        wqT_bf[:, k * DC:(k + 1) * DC],
                        start=(k == 0), stop=(k == 1)),
                        reads=[("ch", "dst"), ("wqT",)], writes=[("pq",)])
                for k in range(2):
                    rec("PE", lambda eng, k=k: eng.matmul(
                        pmm1[:, 0:2 * DC], chT[:, k * 128:(k + 1) * 128],
                        wkvT_bf[:, k * 2 * DC:(k + 1) * 2 * DC],
                        start=(k == 0), stop=(k == 1)),
                        reads=[("ch", "dst"), ("wkvT",)], writes=[("pmm1",)])

                def rope_apply(psrc, pkey, cos_off, sin_off, dst_bf, dkey, rt):
                    rec("DVE", lambda eng: eng.tensor_tensor(
                        out=rtmp[:, 0:128], in0=psrc[:, 128:256],
                        in1=rt[:, sin_off:sin_off + 128], op=ALU.mult),
                        reads=[pkey, rkey], writes=[("c_sb",)])
                    rec("DVE", lambda eng: eng.tensor_tensor(
                        out=rtmp[:, 128:256], in0=psrc[:, 0:128],
                        in1=rt[:, sin_off + 128:sin_off + 256], op=ALU.mult),
                        reads=[pkey, rkey], writes=[("c_sb",)])
                    rec("DVE", lambda eng: eng.tensor_tensor(
                        out=rcos[:], in0=psrc[:, 0:256],
                        in1=rt[:, cos_off:cos_off + 256], op=ALU.mult),
                        reads=[pkey, rkey], writes=[("t1",)])
                    rec("DVE", lambda eng: eng.tensor_tensor(
                        out=dst_bf[:], in0=rcos[:], in1=rtmp[:], op=ALU.add),
                        reads=[("t1",), ("c_sb",)], writes=[dkey])

                rope_apply(pq, ("pq",), 0, DC, qr_bf, ("qr", "src"), rope_t)
                rope_apply(pmm1, ("pmm1",), 2 * DC, 3 * DC, kr_bf, ("kr", "src"), rope_t)
                rec("ACT", lambda eng, i=i: eng.copy(
                    out=v_sb[:, i * DC:(i + 1) * DC], in_=pmm1[:, DC:2 * DC]),
                    reads=[("pmm1",)], writes=[("v", i)])

                transpose_burst(rec, qr_bf, 2, qrTb[par], 0, "qr", pt_a, ("pt", 0),
                                dkey=("qrT", par))
                for kc in range(2):
                    rec("PE", lambda eng, kc=kc: eng.transpose(
                        pt_a[:, kc * 128:(kc + 1) * 128],
                        kr_bf[:, kc * 128:(kc + 1) * 128], ident_sb[:]),
                        reads=[("kr", "src"), ("ident",)], writes=[("pt", 0)])
                for kc, kT in ((0, kT0), (1, kT1)):
                    rec("DVE", lambda eng, kc=kc, kT=kT, i=i: eng.tensor_copy(
                        out=kT[:, i * 128:(i + 1) * 128], in_=pt_a[:, kc * 128:(kc + 1) * 128]),
                        reads=[("pt", 0)], writes=[("kT", kc, i)])

            def back_tile(i, rec):
                par = i % 2
                x_t = x_sb[par]
                xkey = ("x", par)
                qrT = qrTb[par]

                w_keys = (i + 1) * 128
                nch = (w_keys + 511) // 512
                for j in range(nch):
                    ncols = min(512, w_keys - j * 512)
                    ps = (pmm2, p_o)[j % 2]
                    pskey = ("pmm2",) if j % 2 == 0 else ("p_o",)
                    kreads = [("kT", kc, t) for kc in range(2)
                              for t in range(j * 4, min((j * 512 + ncols + 127) // 128, i + 1))]
                    for kc, kT in ((0, kT0), (1, kT1)):
                        rec("PE", lambda eng, j=j, ncols=ncols, kc=kc, kT=kT, ps=ps, qrT=qrT: eng.matmul(
                            ps[:, 0:ncols], qrT[:, kc * 128:(kc + 1) * 128],
                            kT[:, j * 512:j * 512 + ncols],
                            start=(kc == 0), stop=(kc == 1)),
                            reads=[("qrT", par)] + kreads, writes=[pskey])
                    if j == nch - 1:
                        if ncols > 128:
                            rec("ACT", lambda eng, j=j, ncols=ncols, ps=ps: eng.copy(
                                out=scores[:, j * 512:j * 512 + ncols - 128],
                                in_=ps[:, 0:ncols - 128]),
                                reads=[pskey], writes=[("scores",)])
                        rec("DVE", lambda eng, j=j, ncols=ncols, ps=ps: eng.scalar_tensor_tensor(
                            out=scores[:, j * 512 + ncols - 128:j * 512 + ncols],
                            in0=ps[:, ncols - 128:ncols], scalar=0.0, in1=mask_sb[:],
                            op0=ALU.add, op1=ALU.add),
                            reads=[pskey, ("mask",)], writes=[("scores",)])
                    else:
                        if j % 2 == 0:
                            rec("DVE", lambda eng, j=j, ncols=ncols, ps=ps: eng.tensor_copy(
                                out=scores[:, j * 512:j * 512 + ncols], in_=ps[:, 0:ncols]),
                                reads=[pskey], writes=[("scores",)])
                        else:
                            rec("ACT", lambda eng, j=j, ncols=ncols, ps=ps: eng.copy(
                                out=scores[:, j * 512:j * 512 + ncols], in_=ps[:, 0:ncols]),
                                reads=[pskey], writes=[("scores",)])

                rec("DVE", lambda eng, w_keys=w_keys: eng.reduce_max(
                    out=colB(C_NEGMAX), in_=scores[:, 0:w_keys], axis=AX.X, negate=True),
                    reads=[("scores",)], writes=[("colsB",)])
                rec("ACT", lambda eng, w_keys=w_keys: eng.activation(
                    out=exp_bf[:, 0:w_keys], in_=scores[:, 0:w_keys], func=AF.Exp,
                    bias=colB(C_NEGMAX), scale=1.0, accum_out=colB(C_SUME)),
                    reads=[("scores",), ("colsB",)], writes=[("exp", "src"), ("colsB2",)])
                rec("DVE", lambda eng: eng.reciprocal(
                    out=colB(C_RECIP), in_=colB(C_SUME)),
                    reads=[("colsB2",)], writes=[("colsB",)])

                transpose_burst(rec, exp_bf, i + 1, expT, 0, "exp", pt_b, ("pt", 1))
                for b in range(i + 1):
                    rec("PE", lambda eng, b=b, i=i: eng.matmul(
                        pmm2[:, 0:DC], expT[:, b * 128:(b + 1) * 128],
                        v_sb[:, b * DC:(b + 1) * DC],
                        start=(b == 0), stop=(b == i)),
                        reads=[("exp", "dst"), ("v", b)], writes=[("pmm2",)])
                rec("DVE", lambda eng: eng.tensor_scalar(
                    out=o_bf[:], in0=pmm2[:, 0:DC], scalar1=colB(C_RECIP), scalar2=None,
                    op0=ALU.mult), reads=[("pmm2",), ("colsB",)], writes=[("o", "src")])

                transpose_burst(rec, o_bf, 2, oT, 0, "o", pt_b, ("pt", 1))
                for k in range(2):
                    rec("PE", lambda eng, k=k: eng.matmul(
                        p_o[:, 0:DL], oT[:, k * 128:(k + 1) * 128],
                        wc2lT_bf[:, k * DL:(k + 1) * DL],
                        start=(k == 0), stop=(k == 1)),
                        reads=[("o", "dst"), ("wc2lT",)], writes=[("p_o",)])
                rec("ACT", lambda eng: eng.copy(out=z_bf[:], in_=p_o[:, 0:DL]),
                            reads=[("p_o",)], writes=[("z", "src")])
                transpose_burst(rec, z_bf, 4, zT, 0, "z", pt_b, ("pt", 1))

                rec("DVE", lambda eng, x_t=x_t: eng.tensor_tensor(
                    out=xb_sb[:], in0=x_t[:], in1=bout_bc[:], op=ALU.add),
                    reads=[xkey, ("bout_bc",)], writes=[("xb",)])
                for h in range(2):
                    for s in range(2):
                        for k in range(4):
                            rec("PE", lambda eng, h=h, s=s, k=k: eng.matmul(
                                pout2[:, s * 512:(s + 1) * 512],
                                zT[:, k * 128:(k + 1) * 128],
                                wcomb_bf[:, k * D + h * 1024 + s * 512:k * D + h * 1024 + (s + 1) * 512],
                                start=(k == 0), stop=(k == 3)),
                                reads=[("z", "dst"), ("wcomb",)], writes=[("pout2",)])
                    rec("DVE", lambda eng, h=h: eng.tensor_tensor(
                        out=r1_sb[:, h * 1024:(h + 1) * 1024], in0=pout2[:],
                        in1=xb_sb[:, h * 1024:(h + 1) * 1024], op=ALU.add),
                        reads=[("pout2",), ("xb",)], writes=[("r1",)])

                rec("DVE", lambda eng: eng.tensor_tensor(
                    out=xb_sb[:], in0=r1_sb[:], in1=sdownb_bc[:], op=ALU.add),
                    reads=[("r1",), ("sdownb_bc",)], writes=[("xb",)])
                rec("SP", lambda eng, i=i: eng.dma_start(
                    out=out_h[i * 128:(i + 1) * 128, :], in_=xb_sb[:]),
                    reads=[("xb",)], writes=[("out", i)], is_dma=True)

                ln_cols(rec, ("r1",), r1_sb[:], D, exp_bf[:, 0:D], ("exp", "src"),
                        colB, ("colsB",), ("colsB2",))
                rec("DVE", lambda eng: eng.tensor_scalar(
                    out=ln2_bf[:], in0=r1_sb[:], scalar1=colB(C_MEAN), scalar2=colB(C_RSTD),
                    op0=ALU.subtract, op1=ALU.mult),
                    reads=[("r1",), ("colsB",)], writes=[("ln2", "src")])
                rec("SP", lambda eng, i=i: eng.dma_start(
                    out=ln2tok[i * 128:(i + 1) * 128, :], in_=ln2_bf[:]),
                    reads=[("ln2", "src")], writes=[("ln2tok", i)], is_dma=True)

                g4 = i % 4
                done = 0
                while done < 16:
                    for b in range(8):
                        rec("PE", lambda eng, b=b, done=done: eng.transpose(
                            pt_b[:, b * 128:(b + 1) * 128],
                            ln2_bf[:, (done + b) * 128:(done + b + 1) * 128],
                            ident_sb[:]),
                            reads=[("ln2", "src"), ("ident",)], writes=[("pt", 1)])
                    rec("DVE", lambda eng, done=done, g4=g4: eng.tensor_copy(
                        out=stage[:].rearrange("p (c t) -> p c t", c=16)[:, done:done + 8, g4 * 128:(g4 + 1) * 128],
                        in_=pt_b[:].rearrange("p (c t) -> p c t", c=8)),
                        reads=[("pt", 1)], writes=[("stage",)])
                    done += 8

                for k in range(16):
                    rec("PE", lambda eng, k=k, g4=g4: eng.matmul(
                        pout2[:, 0:E], stage[:, k * 512 + g4 * 128:k * 512 + (g4 + 1) * 128],
                        routerT_bf[:, k * E:(k + 1) * E],
                        start=(k == 0), stop=(k == 15)),
                        reads=[("stage",), ("routerT",)], writes=[("pout2",)])
                rec("DVE", lambda eng: eng.tensor_copy(out=r8a[:], in_=pout2[:, 0:E]),
                            reads=[("pout2",)], writes=[("r8a",)])
                rec("DVE", lambda eng: eng.reduce_max(
                    out=colB(C_NM8), in_=r8a[:], axis=AX.X, negate=True),
                    reads=[("r8a",)], writes=[("colsB",)])
                rec("ACT", lambda eng: eng.activation(
                    out=r8b[:], in_=r8a[:], func=AF.Exp, bias=colB(C_NM8), scale=1.0,
                    accum_out=colB(C_S8)),
                    reads=[("r8a",), ("colsB",)], writes=[("r8b",), ("colsB2",)])
                rec("DVE", lambda eng: eng.reciprocal(out=colB(C_R8), in_=colB(C_S8)),
                            reads=[("colsB2",)], writes=[("colsB",)])
                rec("DVE", lambda eng: eng.tensor_scalar(
                    out=r8a[:], in0=r8b[:], scalar1=colB(C_R8), scalar2=None, op0=ALU.mult),
                    reads=[("r8b",), ("colsB",)], writes=[("r8a",)])
                rec("DVE", lambda eng: eng.reduce_max(
                    out=colB(C_M1), in_=r8a[:], axis=AX.X),
                    reads=[("r8a",)], writes=[("colsB",)])
                rec("DVE", lambda eng, i=i: eng.tensor_scalar(
                    out=eq1_sb[:, i * E:(i + 1) * E], in0=r8a[:], scalar1=colB(C_M1),
                    scalar2=None, op0=ALU.is_equal),
                    reads=[("r8a",), ("colsB",)], writes=[("eq1", i)])
                rec("DVE", lambda eng, i=i: eng.tensor_copy(
                    out=r8b[:], in_=eq1_sb[:, i * E:(i + 1) * E]),
                    reads=[("eq1", i)], writes=[("r8b",)])
                rec("DVE", lambda eng: eng.tensor_scalar(
                    out=r8c[:], in0=r8b[:], scalar1=-1.0, scalar2=1.0, op0=ALU.mult, op1=ALU.add),
                    reads=[("r8b",)], writes=[("r8c",)])
                rec("DVE", lambda eng: eng.tensor_tensor(
                    out=r8c[:], in0=r8a[:], in1=r8c[:], op=ALU.mult),
                    reads=[("r8a",), ("r8c",)], writes=[("r8c",)])
                rec("DVE", lambda eng: eng.reduce_max(
                    out=colB(C_M2), in_=r8c[:], axis=AX.X),
                    reads=[("r8c",)], writes=[("colsB",)])
                rec("DVE", lambda eng, i=i: eng.tensor_scalar(
                    out=eq2_sb[:, i * E:(i + 1) * E], in0=r8c[:], scalar1=colB(C_M2),
                    scalar2=None, op0=ALU.is_equal),
                    reads=[("r8c",), ("colsB",)], writes=[("eq2", i)])
                rec("DVE", lambda eng, i=i: eng.tensor_copy(
                    out=r8c[:], in_=eq2_sb[:, i * E:(i + 1) * E]),
                    reads=[("eq2", i)], writes=[("r8c",)])
                rec("DVE", lambda eng: eng.tensor_tensor(
                    out=r8b[:], in0=r8b[:], in1=r8c[:], op=ALU.add),
                    reads=[("r8b",), ("r8c",)], writes=[("r8b",)])
                rec("DVE", lambda eng: eng.tensor_tensor(
                    out=r8b[:], in0=r8a[:], in1=r8b[:], op=ALU.mult),
                    reads=[("r8a",), ("r8b",)], writes=[("r8b",)])
                rec("DVE", lambda eng: eng.tensor_tensor(
                    out=colB(C_DEN), in0=colB(C_M1), in1=colB(C_M2), op=ALU.add),
                    reads=[("colsB",)], writes=[("colsB",)])
                rec("DVE", lambda eng: eng.reciprocal(out=colB(C_RDEN), in_=colB(C_DEN)),
                            reads=[("colsB",)], writes=[("colsB",)])
                rec("DVE", lambda eng, i=i: eng.tensor_scalar(
                    out=ce_sb[:, i * E:(i + 1) * E], in0=r8b[:], scalar1=colB(C_RDEN),
                    scalar2=None, op0=ALU.mult),
                    reads=[("r8b",), ("colsB",)], writes=[("ce", i)])
                rec("DVE", lambda eng, i=i: eng.tensor_tensor(
                    out=r8b[:], in0=eq1_sb[:, i * E:(i + 1) * E],
                    in1=ce_sb[:, i * E:(i + 1) * E], op=ALU.mult),
                    reads=[("eq1", i), ("ce", i)], writes=[("r8b",)])
                rec("DVE", lambda eng, i=i: eng.reduce_sum(
                    out=ce12_sb[:, 2 * i:2 * i + 1], in_=r8b[:], axis=AX.X),
                    reads=[("r8b",)], writes=[("ce12", i)])
                rec("DVE", lambda eng, i=i: eng.tensor_tensor(
                    out=r8b[:], in0=eq2_sb[:, i * E:(i + 1) * E],
                    in1=ce_sb[:, i * E:(i + 1) * E], op=ALU.mult),
                    reads=[("eq2", i), ("ce", i)], writes=[("r8b",)])
                rec("DVE", lambda eng, i=i: eng.reduce_sum(
                    out=ce12_sb[:, 2 * i + 1:2 * i + 2], in_=r8b[:], axis=AX.X),
                    reads=[("r8b",)], writes=[("ce12", i)])
                rec("DVE", lambda eng, i=i: eng.tensor_tensor(
                    out=sel8_bf[:], in0=eq1_sb[:, i * E:(i + 1) * E],
                    in1=eq2_sb[:, i * E:(i + 1) * E], op=ALU.add),
                    reads=[("eq1", i), ("eq2", i)], writes=[("sel8",)])
                rec("PE", lambda eng: eng.matmul(
                    pout2[:, 0:E], cumtri_sb[:], sel8_bf[:], start=True, stop=True),
                    reads=[("sel8",), ("cumtri",)], writes=[("pout2",)])
                rec("PE", lambda eng: eng.matmul(
                    pout2[0:1, E:2 * E], ones128b_sb[:], sel8_bf[:], start=True, stop=True),
                    reads=[("sel8",), ("ones128b",)], writes=[("pout2",)])
                rec("PE", lambda eng: eng.matmul(
                    pout2[:, 2 * E:3 * E], ones_sb[0:1, :], offs_sb[:], start=True, stop=True),
                    reads=[("offs",), ("ones",)], writes=[("pout2",)])
                rec("DVE", lambda eng: eng.tensor_copy(
                    out=et8[:], in_=pout2[:, 2 * E:3 * E]),
                    reads=[("pout2",)], writes=[("et8",)])
                rec("DVE", lambda eng: eng.tensor_tensor(
                    out=r8c[:], in0=pout2[:, 0:E], in1=et8[:], op=ALU.add),
                    reads=[("pout2",), ("et8",)], writes=[("r8c",)])
                rec("DVE", lambda eng, i=i: eng.tensor_tensor(
                    out=r8b[:], in0=eq1_sb[:, i * E:(i + 1) * E], in1=r8c[:], op=ALU.mult),
                    reads=[("eq1", i), ("r8c",)], writes=[("r8b",)])
                rec("DVE", lambda eng, i=i: eng.reduce_sum(
                    out=gr12_sb[:, 2 * i:2 * i + 1], in_=r8b[:], axis=AX.X),
                    reads=[("r8b",)], writes=[("gr12", i)])
                rec("DVE", lambda eng, i=i: eng.tensor_tensor(
                    out=r8b[:], in0=eq2_sb[:, i * E:(i + 1) * E], in1=r8c[:], op=ALU.mult),
                    reads=[("eq2", i), ("r8c",)], writes=[("r8b",)])
                rec("DVE", lambda eng, i=i: eng.reduce_sum(
                    out=gr12_sb[:, 2 * i + 1:2 * i + 2], in_=r8b[:], axis=AX.X),
                    reads=[("r8b",)], writes=[("gr12", i)])
                rec("DVE", lambda eng: eng.tensor_tensor(
                    out=offs_sb[:], in0=offs_sb[:], in1=pout2[0:1, E:2 * E], op=ALU.add),
                    reads=[("offs",), ("pout2",)], writes=[("offs",)])

                if g4 == 3:
                    rec("SP", lambda eng, i=i: eng.dma_start(
                        out=ln2h[:, :, (i - 3) * 128:(i + 1) * 128].rearrange("c p t -> p c t"),
                        in_=stage[:].rearrange("p (c t) -> p c t", c=16)),
                        reads=[("stage",)], writes=[("ln2h", i // 4)], is_dma=True)

            def collect(fn, i):
                L = []
                fn(i, lambda *a, **k: L.append((a, k)))
                return L

            def merge_emit(A, B):
                na, nb = len(A), len(B)
                ia = ib = 0
                while ia < na or ib < nb:
                    if ib >= nb or (ia < na and ia * nb <= ib * na):
                        a, k = A[ia]; ia += 1
                    else:
                        a, k = B[ib]; ib += 1
                    prog.record(*a, **k)

            for a, k in collect(front_tile, 0):
                prog.record(*a, **k)
            for i in range(n_tiles):
                A = collect(front_tile, i + 1) if i + 1 < n_tiles else []
                merge_emit(A, collect(back_tile, i))

            if sparse:
                prog.record("DVE", lambda eng: eng.memset(trow[0:1, 8:16], 0.0),
                            writes=[("trow",)])
                for j_ in range(8):
                    prog.record("DVE", lambda eng, j_=j_: eng.tensor_scalar(
                        out=trow[0:1, 24:32], in0=offs_sb[:], scalar1=float(512 * j_),
                        scalar2=None, op0=ALU.subtract),
                        reads=[("offs",)], writes=[("trow",)])
                    prog.record("ACT", lambda eng: eng.activation(
                        out=trow[0:1, 24:32], in_=trow[0:1, 24:32], func=AF.Sign,
                        bias=0.0, scale=1.0),
                        reads=[("trow",)], writes=[("trow",)])
                    prog.record("DVE", lambda eng: eng.tensor_scalar(
                        out=trow[0:1, 24:32], in0=trow[0:1, 24:32], scalar1=0.0,
                        scalar2=None, op0=ALU.max),
                        reads=[("trow",)], writes=[("trow",)])
                    prog.record("DVE", lambda eng: eng.tensor_tensor(
                        out=trow[0:1, 8:16], in0=trow[0:1, 8:16], in1=trow[0:1, 24:32],
                        op=ALU.add),
                        reads=[("trow",)], writes=[("trow",)])
                prog.record("DVE", lambda eng: eng.tensor_scalar(
                    out=trow[0:1, 16:24], in0=trow[0:1, 8:16], scalar1=512.0,
                    scalar2=None, op0=ALU.mult),
                    reads=[("trow",)], writes=[("trow",)])
                prog.record("DVE", lambda eng: eng.memset(trow[0:1, 0:8], 0.0),
                            writes=[("trow",)])
                for e_ in range(1, E):
                    prog.record("DVE", lambda eng, e_=e_: eng.tensor_tensor(
                        out=trow[0:1, e_:e_ + 1], in0=trow[0:1, e_ - 1:e_],
                        in1=trow[0:1, 16 + e_ - 1:16 + e_], op=ALU.add),
                        reads=[("trow",)], writes=[("trow",)])
                prog.record("PE", lambda eng: eng.matmul(
                    pout2[:, 24:32], ones_sb[0:1, :], trow[0:1, 0:8], start=True, stop=True),
                    reads=[("trow",), ("ones",)], writes=[("pout2",)])
                prog.record("DVE", lambda eng: eng.tensor_copy(
                    out=base_bc_sb[:], in_=pout2[:, 24:32]),
                    reads=[("pout2",)], writes=[("base_bc",)])
                prog.record("PE", lambda eng: eng.matmul(
                    pout2[0:24, 24:32], ones_sb[0:1, 0:24], trow[0:1, 0:8], start=True, stop=True),
                    reads=[("trow",), ("ones",)], writes=[("pout2",)])
                prog.record("DVE", lambda eng: eng.tensor_scalar(
                    out=et8[0:24, :], in0=pout2[0:24, 24:32],
                    scalar1=starts24_sb[0:24, 0:1], scalar2=None, op0=ALU.subtract),
                    reads=[("pout2",), ("starts24",)], writes=[("et8",)])
                prog.record("ACT", lambda eng: eng.activation(
                    out=et8[0:24, :], in_=et8[0:24, :], func=AF.Sign, bias=0.0, scale=1.0),
                    reads=[("et8",)], writes=[("et8",)])
                prog.record("DVE", lambda eng: eng.tensor_scalar(
                    out=et8[0:24, :], in0=et8[0:24, :], scalar1=0.0, scalar2=None,
                    op0=ALU.max),
                    reads=[("et8",)], writes=[("et8",)])
                prog.record("DVE", lambda eng: eng.tensor_scalar(
                    out=et8[0:24, :], in0=et8[0:24, :], scalar1=-1.0, scalar2=1.0,
                    op0=ALU.mult, op1=ALU.add),
                    reads=[("et8",)], writes=[("et8",)])
                prog.record("DVE", lambda eng: eng.reduce_sum(
                    out=e_colv[0:24, 0:1], in_=et8[0:24, :], axis=AX.X),
                    reads=[("et8",)], writes=[("e_colv",)])
                prog.record("DVE", lambda eng: eng.tensor_scalar(
                    out=e_colv[0:24, 0:1], in0=e_colv[0:24, 0:1], scalar1=-1.0,
                    scalar2=None, op0=ALU.add),
                    reads=[("e_colv",)], writes=[("e_colv",)])
                prog.record("SP", lambda eng: eng.dma_start(
                    out=e_dram[0:1, :].rearrange("r c -> c r"), in_=e_colv[0:24, 0:1]),
                    reads=[("e_colv",)], writes=[("e_dram",)], is_dma=True)
                from concourse.bass_core_rust import IndirectOffsetOnAxis
                for i in range(n_tiles):
                    par = i % 2
                    srt = ln2_bf if par == 0 else exp_bf
                    skey = ("ln2", "src") if par == 0 else ("exp", "src")
                    prog.record("SP", lambda eng, i=i, srt=srt: eng.dma_start(
                        out=srt[:, 0:D], in_=ln2tok[i * 128:(i + 1) * 128, :]),
                        reads=[("ln2tok", i)], writes=[skey], is_dma=True)
                    for which in (0, 1):
                        eqs = eq1_sb if which == 0 else eq2_sb
                        eqk = ("eq1", i) if which == 0 else ("eq2", i)
                        prog.record("DVE", lambda eng, i=i, eqs=eqs: eng.tensor_tensor(
                            out=r8b[:], in0=eqs[:, i * E:(i + 1) * E], in1=base_bc_sb[:],
                            op=ALU.mult),
                            reads=[eqk, ("base_bc",)], writes=[("r8b",)])
                        prog.record("DVE", lambda eng, which=which: eng.reduce_sum(
                            out=dcol[:, which:which + 1], in_=r8b[:], axis=AX.X),
                            reads=[("r8b",)], writes=[("dcol",)])
                        prog.record("DVE", lambda eng, i=i, which=which: eng.tensor_scalar(
                            out=dcol[:, 2 + which:3 + which], in0=dcol[:, which:which + 1],
                            scalar1=gr12_sb[:, 2 * i + which:2 * i + which + 1],
                            scalar2=-1.0, op0=ALU.add, op1=ALU.add),
                            reads=[("dcol",), ("gr12", i)], writes=[("dcol",)])
                        prog.record("DVE", lambda eng, i=i, which=which: eng.tensor_copy(
                            out=dst_sb[:, 2 * i + which:2 * i + which + 1],
                            in_=dcol[:, 2 + which:3 + which]),
                            reads=[("dcol",)], writes=[("dst", i)])
                        prog.record("POOL", lambda eng, i=i, which=which, srt=srt: eng.indirect_dma_start(
                            out=sorted_h[:],
                            out_offset=IndirectOffsetOnAxis(dst_sb[:, 2 * i + which:2 * i + which + 1], 0),
                            in_=srt[:, 0:D], in_offset=None),
                            reads=[skey, ("dst", i)], writes=[("sorted",)], is_dma=True)

            prog.flush()

        if run_phase_b:
            with ExitStack() as pb:
                def sbuf(name, shape, dt):
                    return pb.enter_context(nc.sbuf_tensor(name, shape, dt))

                if sparse:
                    gup_sb = [sbuf(f"gup_sb{p}", [128, 16 * 1536], BF16) for p in range(2)]
                    gate_sb = [gup_sb[0][:, 0:16 * 768], gup_sb[1][:, 0:16 * 768]]
                    upw_sb = [gup_sb[0][:, 16 * 768:32 * 768], gup_sb[1][:, 16 * 768:32 * 768]]
                else:
                    gate_sb = [sbuf(f"gate_sb{p}", [128, 16 * 768], BF16) for p in range(2)]
                    upw_sb = [sbuf(f"upw_sb{p}", [128, 16 * 768], BF16) for p in range(2)]
                down_sb = [sbuf(f"down_sb{p}", [128, 6 * D], BF16) for p in range(2)]
                ln2g = [sbuf(f"ln2g{p}", [128, 16 * 512], BF16) for p in range(2)]
                act_bf = sbuf("act_bf", [128, 6 * 512], BF16)
                silu_sb = [sbuf(f"silu_sb{p}", [128, 512], F32) for p in range(2)]
                y_sb = [sbuf(f"y_sb{p}", [128, D], F32) for p in range(2)]
                sgb_sb = sbuf("sgb_sb", [128, 22], F32)
                sub_sb = sbuf("sub_sb", [128, 22], F32)
                pg = [pb.enter_context(nc.psum_tensor(f"pg{p}", [128, 512], F32)) for p in range(2)]
                pu = [pb.enter_context(nc.psum_tensor("pu0", [128, 512], F32))] * 2
                pyA = pb.enter_context(nc.psum_tensor("pyA", [128, 1024], F32))
                pyB = pb.enter_context(nc.psum_tensor("pyB", [128, 1024], F32))

                prog.record("SP", lambda eng: eng.dma_start(out=sgb_sb[:], in_=sgb_h[:]),
                            writes=[("sgb",)], is_dma=True)
                prog.record("SP", lambda eng: eng.dma_start(out=sub_sb[:], in_=sub_h[:]),
                            writes=[("sub",)], is_dma=True)

                passes = []
                if not sparse:
                    for e in moe_experts:
                        passes.append(("moe", e, 0, 6))
                        passes.append(("moe", e, 6, 5))
                for c0, cn in shared_chunks:
                    passes.append(("sh", None, c0, cn))

                def load_pass_weights(p_idx):
                    kind, e, ic0, icn = passes[p_idx]
                    par = p_idx % 2
                    W = icn * 128
                    g_dst = gate_sb[par].rearrange("p (k w) -> p k w", k=16)[:, :, 0:W]
                    u_dst = upw_sb[par].rearrange("p (k w) -> p k w", k=16)[:, :, 0:W]
                    d_dst = down_sb[par][:].rearrange("p (c h) -> p c h", c=6)[:, 0:icn, :]
                    if kind == "moe":
                        g_src = gate_h[e, :, ic0 * 128:(ic0 + icn) * 128].rearrange("(k p) w -> p k w", p=128)
                        u_src = up_h[e, :, ic0 * 128:(ic0 + icn) * 128].rearrange("(k p) w -> p k w", p=128)
                        d_src = down_h[e, ic0 * 128:(ic0 + icn) * 128, :].rearrange("(c p) h -> p c h", p=128)
                    else:
                        gsh = sgate_bf if sparse else sgate_h
                        ush = sup_bf if sparse else sup_h
                        dsh = sdown_bf if sparse else sdown_h
                        g_src = gsh[:, ic0 * 128:(ic0 + icn) * 128].rearrange("(k p) w -> p k w", p=128)
                        u_src = ush[:, ic0 * 128:(ic0 + icn) * 128].rearrange("(k p) w -> p k w", p=128)
                        d_src = dsh[ic0 * 128:(ic0 + icn) * 128, :].rearrange("(c p) h -> p c h", p=128)
                    wq = "SP" if (kind == "sh" and sparse) else "POOL"
                    prog.record(wq, lambda eng: eng.dma_start(out=g_dst, in_=g_src),
                                writes=[("gup", par, k_) for k_ in range(8)], is_dma=True)
                    prog.record(wq, lambda eng: eng.dma_start(out=u_dst, in_=u_src),
                                writes=[("gup", par, k_) for k_ in range(8, 16)], is_dma=True)
                    prog.record(wq, lambda eng: eng.dma_start(out=d_dst, in_=d_src),
                                writes=[("dnc", par, k_) for k_ in range(6)], is_dma=True)

                def load_group(p_idx, g):
                    gpar = g % 2
                    prog.record("SP", lambda eng, g=g, gpar=gpar: eng.dma_start(
                        out=ln2g[gpar][:].rearrange("p (c t) -> p c t", c=16),
                        in_=ln2h[:, :, g * 512:(g + 1) * 512].rearrange("c p t -> p c t")),
                        reads=[("ln2h", g)], writes=[("ln2g", gpar)], is_dma=True)

                if passes:
                    load_pass_weights(0)
                for p_idx, (kind, e, ic0, icn) in enumerate(passes):
                    par = p_idx % 2
                    if p_idx + 1 < len(passes):
                        load_pass_weights(p_idx + 1)
                    W = icn * 128
                    load_group(p_idx, 0)
                    for g in range(n_groups):
                        gpar = g % 2
                        if g + 1 < n_groups:
                            load_group(p_idx, g + 1)
                        for c in range(icn):
                            cp = c % 2
                            for k in range(16):
                                prog.record("PE", lambda eng, c=c, k=k, cp=cp, par=par, gpar=gpar: eng.matmul(
                                    pg[cp][:, 0:512],
                                    gate_sb[par][:, k * 768 + c * 128:k * 768 + (c + 1) * 128],
                                    ln2g[gpar][:, k * 512:(k + 1) * 512],
                                    start=(k == 0), stop=(k == 15)),
                                    reads=[("gup", par, 0), ("gup", par, 7), ("ln2g", gpar)], writes=[("pg", cp)])
                            for k in range(16):
                                prog.record("PE", lambda eng, c=c, k=k, cp=cp, par=par, gpar=gpar: eng.matmul(
                                    pu[cp][:, 0:512],
                                    upw_sb[par][:, k * 768 + c * 128:k * 768 + (c + 1) * 128],
                                    ln2g[gpar][:, k * 512:(k + 1) * 512],
                                    start=(k == 0), stop=(k == 15)),
                                    reads=[("gup", par, 8), ("gup", par, 15), ("ln2g", gpar)], writes=[("pu",)])
                            if kind == "sh":
                                gc = ic0 + c
                                prog.record("ACT", lambda eng, cp=cp, gc=gc: eng.activation(
                                    out=silu_sb[cp][:], in_=pg[cp][:, 0:512], func=AF.Silu,
                                    bias=sgb_sb[:, gc:gc + 1], scale=1.0),
                                    reads=[("pg", cp), ("sgb",)], writes=[("silu", cp)])
                                prog.record("DVE", lambda eng, c=c, cp=cp, gc=gc: eng.scalar_tensor_tensor(
                                    out=act_bf[:, c * 512:(c + 1) * 512], in0=pu[cp][:, 0:512],
                                    scalar=sub_sb[:, gc:gc + 1], in1=silu_sb[cp][:],
                                    op0=ALU.add, op1=ALU.mult),
                                    reads=[("pu",), ("silu", cp), ("sub",)], writes=[("act", c)])
                            else:
                                prog.record("ACT", lambda eng, cp=cp: eng.activation(
                                    out=silu_sb[cp][:], in_=pg[cp][:, 0:512], func=AF.Silu,
                                    bias=0.0, scale=1.0),
                                    reads=[("pg", cp)], writes=[("silu", cp)])
                                prog.record("DVE", lambda eng, c=c, cp=cp: eng.scalar_tensor_tensor(
                                    out=act_bf[:, c * 512:(c + 1) * 512], in0=pu[cp][:, 0:512],
                                    scalar=0.0, in1=silu_sb[cp][:],
                                    op0=ALU.add, op1=ALU.mult),
                                    reads=[("pu",), ("silu", cp)], writes=[("act", c)])
                        for t in range(4):
                            gt = g * 4 + t
                            for c in range(icn):
                                for hc in range(4):
                                    pyT = pyA if hc < 2 else pyB
                                    pykey = ("pyA",) if hc < 2 else ("pyB",)
                                    prog.record("PE", lambda eng, t=t, c=c, hc=hc, icn=icn, par=par, pyT=pyT: eng.matmul(
                                        pyT[:, (hc % 2) * 512:(hc % 2 + 1) * 512],
                                        act_bf[:, c * 512 + t * 128:c * 512 + (t + 1) * 128],
                                        down_sb[par][:, c * D + hc * 512:c * D + (hc + 1) * 512],
                                        start=(c == 0), stop=(c == icn - 1)),
                                        reads=[("act", c), ("dnc", par, c)], writes=[pykey])
                            ypar = t % 2
                            if kind == "moe":
                                prog.record("DVE", lambda eng, gt=gt, e=e, ypar=ypar: eng.tensor_scalar(
                                    out=y_sb[ypar][:, 0:1024], in0=pyA[:],
                                    scalar1=ce_sb[:, gt * E + e:gt * E + e + 1],
                                    scalar2=None, op0=ALU.mult),
                                    reads=[("pyA",), ("ce", gt)], writes=[("y", ypar, 0)])
                                prog.record("ACT", lambda eng, gt=gt, e=e, ypar=ypar: eng.activation(
                                    out=y_sb[ypar][:, 1024:2048], in_=pyB[:], func=AF.Identity,
                                    bias=0.0, scale=ce_sb[:, gt * E + e:gt * E + e + 1]),
                                    reads=[("pyB",), ("ce", gt)], writes=[("y", ypar, 1)])
                            else:
                                prog.record("DVE", lambda eng, ypar=ypar: eng.tensor_copy(
                                    out=y_sb[ypar][:, 0:1024], in_=pyA[:]),
                                    reads=[("pyA",)], writes=[("y", ypar, 0)])
                                prog.record("ACT", lambda eng, ypar=ypar: eng.copy(
                                    out=y_sb[ypar][:, 1024:2048], in_=pyB[:]),
                                    reads=[("pyB",)], writes=[("y", ypar, 1)])
                            prog.record("POOL", lambda eng, gt=gt, ypar=ypar: eng.dma_start(
                                out=out_h[gt * 128:(gt + 1) * 128, :], in_=y_sb[ypar][:],
                                accum_op=ALU.add),
                                reads=[("y", ypar, 0), ("y", ypar, 1), ("out", gt)],
                                writes=[("out", gt)], is_dma=True)

                if sparse:
                    from concourse.bass_core_rust import IndirectOffsetOnAxis
                    prog.flush()
                    ident2_sb = sbuf("ident2_sb", [128, 128], BF16)
                    ones_b_sb = sbuf("ones_b_sb", [1, 128], F32)
                    prog.record("SP", lambda eng: eng.dma_start(out=ident2_sb[:], in_=ident_h[:]),
                                writes=[("ident2",)], is_dma=True)
                    prog.record("SP", lambda eng: eng.dma_start(out=ones_b_sb[:], in_=ones_h[:]),
                                writes=[("ones_b",)], is_dma=True)
                    kp16_sb = sbuf("kp16_sb", [128, 16], F32)
                    kp11_sb = sbuf("kp11_sb", [128, 11], F32)
                    idxf = sbuf("idxf", [128, 27], F32)
                    idx_gu = [sbuf(f"idx_gu{p}", [128, 16], mybir.dt.int32) for p in range(2)]
                    idx_dn = [sbuf(f"idx_dn{p}", [128, 11], mybir.dt.int32) for p in range(2)]
                    ecol2 = sbuf("ecol2", [128, 2], F32)
                    e_bc_all = sbuf("e_bc_all", [128, 24], F32)
                    e_row_sb = sbuf("e_row_sb", [1, 24], F32)
                    lnTs = [ln2g[0], ln2g[1]]
                    down_view = downc_bf

                    prog.record("SP", lambda eng: eng.dma_start(out=kp16_sb[:], in_=kp16_h[:]),
                                writes=[("kp16",)], is_dma=True)
                    prog.record("SP", lambda eng: eng.dma_start(out=kp11_sb[:], in_=kp11_h[:]),
                                writes=[("kp11",)], is_dma=True)
                    prog.record("SP", lambda eng: eng.dma_start(out=e_row_sb[:], in_=e_dram[:]),
                                writes=[("e_row",)], is_dma=True)
                    prog.record("PE", lambda eng: eng.matmul(
                        pg[0][:, 0:24], ones_b_sb[0:1, :], e_row_sb[:], start=True, stop=True),
                        reads=[("e_row",), ("ones_b",)], writes=[("pg", 0)])
                    prog.record("DVE", lambda eng: eng.tensor_copy(
                        out=e_bc_all[:], in_=pg[0][:, 0:24]),
                        reads=[("pg", 0)], writes=[("e_bc",)])

                    def sp_gather(p_idx):
                        s_, h_ = divmod(p_idx, 2)
                        par = p_idx % 2
                        icn = 6 if h_ == 0 else 5
                        W = icn * 128
                        gsrc = gupA_bf if h_ == 0 else gupB_bf
                        W2 = 1536 if h_ == 0 else 1280
                        k0 = 0 if h_ == 0 else 6
                        prog.record("DVE", lambda eng, s_=s_: eng.tensor_scalar(
                            out=ecol2[:, 0:1], in0=e_bc_all[:, s_:s_ + 1], scalar1=2048.0,
                            scalar2=None, op0=ALU.mult),
                            reads=[("e_bc",)], writes=[("ecol2",)])
                        prog.record("DVE", lambda eng, s_=s_: eng.tensor_scalar(
                            out=ecol2[:, 1:2], in0=e_bc_all[:, s_:s_ + 1], scalar1=1408.0,
                            scalar2=None, op0=ALU.mult),
                            reads=[("e_bc",)], writes=[("ecol2",)])
                        prog.record("DVE", lambda eng: eng.tensor_scalar(
                            out=idxf[:, 0:16], in0=kp16_sb[:], scalar1=ecol2[:, 0:1],
                            scalar2=None, op0=ALU.add),
                            reads=[("kp16",), ("ecol2",)], writes=[("idxf",)])
                        prog.record("DVE", lambda eng: eng.tensor_scalar(
                            out=idxf[:, 16:27], in0=kp11_sb[:], scalar1=ecol2[:, 1:2],
                            scalar2=None, op0=ALU.add),
                            reads=[("kp11",), ("ecol2",)], writes=[("idxf",)])
                        prog.record("DVE", lambda eng, par=par: eng.tensor_copy(
                            out=idx_gu[par][:], in_=idxf[:, 0:16]),
                            reads=[("idxf",)], writes=[("idxgu", par)])
                        prog.record("DVE", lambda eng, par=par: eng.tensor_copy(
                            out=idx_dn[par][:], in_=idxf[:, 16:27]),
                            reads=[("idxf",)], writes=[("idxdn", par)])
                        for k in range(16):
                            prog.record("POOL", lambda eng, k=k, par=par, W2=W2, gsrc=gsrc: eng.indirect_dma_start(
                                out=gup_sb[par][:, k * 1536:k * 1536 + W2], out_offset=None,
                                in_=gsrc,
                                in_offset=IndirectOffsetOnAxis(idx_gu[par][:, k:k + 1], 0)),
                                reads=[("idxgu", par)], writes=[("gup", par, k)], is_dma=True)
                        for k in range(icn):
                            prog.record("POOL", lambda eng, k=k, k0=k0, par=par: eng.indirect_dma_start(
                                out=down_sb[par][:, k * D:(k + 1) * D], out_offset=None,
                                in_=down_view,
                                in_offset=IndirectOffsetOnAxis(idx_dn[par][:, k0 + k:k0 + k + 1], 0)),
                                reads=[("idxdn", par)], writes=[("dnc", par, k)], is_dma=True)

                    def sp_compute_half(s_, h_):
                        par = (s_ * 2 + h_) % 2
                        icn = 6 if h_ == 0 else 5
                        upoff = 768 if h_ == 0 else 640
                        lnT = lnTs[s_ % 2]
                        for c in range(icn):
                            cp = c % 2
                            for k in range(16):
                                prog.record("PE", lambda eng, c=c, k=k, cp=cp, par=par: eng.matmul(
                                    pg[cp][:, 0:512],
                                    gup_sb[par][:, k * 1536 + c * 128:k * 1536 + (c + 1) * 128],
                                    lnT[:, k * 512:(k + 1) * 512],
                                    start=(k == 0), stop=(k == 15)),
                                    reads=[("gup", par, k), ("lnT", s_ % 2, k)], writes=[("pg", cp)])
                            for k in range(16):
                                prog.record("PE", lambda eng, c=c, k=k, par=par, upoff=upoff: eng.matmul(
                                    pu[0][:, 0:512],
                                    gup_sb[par][:, k * 1536 + upoff + c * 128:k * 1536 + upoff + (c + 1) * 128],
                                    lnT[:, k * 512:(k + 1) * 512],
                                    start=(k == 0), stop=(k == 15)),
                                    reads=[("gup", par, k), ("lnT", s_ % 2, k)], writes=[("pu",)])
                            prog.record("ACT", lambda eng, cp=cp: eng.activation(
                                out=silu_sb[cp][:], in_=pg[cp][:, 0:512], func=AF.Silu,
                                bias=0.0, scale=1.0),
                                reads=[("pg", cp)], writes=[("silu", cp)])
                            prog.record("DVE", lambda eng, c=c, cp=cp: eng.scalar_tensor_tensor(
                                out=act_bf[:, c * 512:(c + 1) * 512], in0=pu[0][:, 0:512],
                                scalar=0.0, in1=silu_sb[cp][:],
                                op0=ALU.add, op1=ALU.mult),
                                reads=[("pu",), ("silu", cp)], writes=[("act", c)])
                        for t in range(4):
                            for c in range(icn):
                                for hc in range(4):
                                    pyT = pyA if hc < 2 else pyB
                                    pykey = ("pyA",) if hc < 2 else ("pyB",)
                                    prog.record("PE", lambda eng, t=t, c=c, hc=hc, icn=icn, par=par, pyT=pyT: eng.matmul(
                                        pyT[:, (hc % 2) * 512:(hc % 2 + 1) * 512],
                                        act_bf[:, c * 512 + t * 128:c * 512 + (t + 1) * 128],
                                        down_sb[par][:, c * D + hc * 512:c * D + (hc + 1) * 512],
                                        start=(c == 0), stop=(c == icn - 1)),
                                        reads=[("act", c), ("dnc", par, c)], writes=[pykey])
                            ypar = t % 2
                            prog.record("DVE", lambda eng, ypar=ypar: eng.tensor_copy(
                                out=y_sb[ypar][:, 0:1024], in_=pyA[:]),
                                reads=[("pyA",)], writes=[("y", ypar, 0)])
                            prog.record("ACT", lambda eng, ypar=ypar: eng.copy(
                                out=y_sb[ypar][:, 1024:2048], in_=pyB[:]),
                                reads=[("pyB",)], writes=[("y", ypar, 1)])
                            r0 = s_ * 512 + t * 128
                            c0 = 0 if h_ == 0 else D
                            prog.record("SP", lambda eng, r0=r0, ypar=ypar, c0=c0: eng.dma_start(
                                out=ysAB[r0:r0 + 128, c0:c0 + D], in_=y_sb[ypar][:]),
                                reads=[("y", ypar, 0), ("y", ypar, 1)],
                                writes=[("ys", s_, t, h_)], is_dma=True)

                    def sp_lnT(s_):
                        buf = lnTs[s_ % 2]
                        for kc in range(16):
                            prog.record("SP", lambda eng, s_=s_, kc=kc, buf=buf: eng.dma_start(
                                out=buf[:, kc * 512:(kc + 1) * 512],
                                in_=sorted_h[s_ * 512:(s_ + 1) * 512, kc * 128:(kc + 1) * 128],
                                transpose=True),
                                reads=[("sorted",)], writes=[("lnT", s_ % 2, kc)], is_dma=True)

                    n_sp = 2 * NST
                    if n_sp:
                        sp_gather(0)
                        if n_sp > 1:
                            sp_gather(1)
                        sp_lnT(0)
                    for p_idx in range(n_sp):
                        s_, h_ = divmod(p_idx, 2)
                        if h_ == 0 and s_ + 1 < NST:
                            sp_lnT(s_ + 1)
                        sp_compute_half(s_, h_)
                        if p_idx + 2 < n_sp:
                            sp_gather(p_idx + 2)

                    prog.flush()
                    g1 = [gup_sb[p][:, 0:8192].bitcast(F32) for p in range(2)]
                    g2 = [gup_sb[p][:, 8192:16384].bitcast(F32) for p in range(2)]
                    yc = [gup_sb[p][:, 16384:20480].bitcast(F32) for p in range(2)]
                    oacc = [down_sb[p][:, 0:4096].bitcast(F32) for p in range(2)]
                    for i in range(n_tiles):
                        par = i % 2
                        prog.record("POOL", lambda eng, i=i, par=par: eng.indirect_dma_start(
                            out=g1[par], out_offset=None, in_=ysAB[:],
                            in_offset=IndirectOffsetOnAxis(dst_sb[:, 2 * i:2 * i + 1], 0)),
                            reads=[("dstf", i)], writes=[("g1", par)], is_dma=True)
                        prog.record("POOL", lambda eng, i=i, par=par: eng.indirect_dma_start(
                            out=g2[par], out_offset=None, in_=ysAB[:],
                            in_offset=IndirectOffsetOnAxis(dst_sb[:, 2 * i + 1:2 * i + 2], 0)),
                            reads=[("dstf", i)], writes=[("g2", par)], is_dma=True)
                        prog.record("DVE", lambda eng, par=par: eng.tensor_tensor(
                            out=g1[par][:, 0:D], in0=g1[par][:, 0:D], in1=g1[par][:, D:2 * D],
                            op=ALU.add),
                            reads=[("g1", par)], writes=[("g1", par)])
                        prog.record("DVE", lambda eng, par=par: eng.tensor_tensor(
                            out=g2[par][:, 0:D], in0=g2[par][:, 0:D], in1=g2[par][:, D:2 * D],
                            op=ALU.add),
                            reads=[("g2", par)], writes=[("g2", par)])
                        prog.record("DVE", lambda eng, i=i, par=par: eng.tensor_scalar(
                            out=yc[par], in0=g1[par][:, 0:D], scalar1=ce12_sb[:, 2 * i:2 * i + 1],
                            scalar2=None, op0=ALU.mult),
                            reads=[("g1", par), ("ce12", i)], writes=[("yc", par)])
                        prog.record("DVE", lambda eng, i=i, par=par: eng.scalar_tensor_tensor(
                            out=yc[par], in0=g2[par][:, 0:D], scalar=ce12_sb[:, 2 * i + 1:2 * i + 2],
                            in1=yc[par], op0=ALU.mult, op1=ALU.add),
                            reads=[("g2", par), ("yc", par), ("ce12", i)], writes=[("yc", par)])
                        prog.record("SP", lambda eng, i=i, par=par: eng.dma_start(
                            out=oacc[par], in_=out_h[i * 128:(i + 1) * 128, :]),
                            reads=[("out", i)], writes=[("oacc", par)], is_dma=True)
                        prog.record("DVE", lambda eng, i=i, par=par: eng.tensor_tensor(
                            out=yc[par], in0=yc[par], in1=oacc[par], op=ALU.add),
                            reads=[("yc", par), ("oacc", par)], writes=[("yc", par)])
                        prog.record("SP", lambda eng, i=i, par=par: eng.dma_start(
                            out=out_h[i * 128:(i + 1) * 128, :], in_=yc[par]),
                            reads=[("yc", par), ("out", i)], writes=[("out", i)], is_dma=True)
                prog.flush()

        prog.finish()
    return nc


def _host_prep(inputs):
    f32 = np.float32
    c = np.ascontiguousarray

    def t(a):
        return c(np.asarray(a, dtype=f32).T)

    wtkv = c(np.asarray(inputs["L0_attn_W_to_kv"], f32))
    shared = {
        "wtkvT": t(wtkv),
        "wtkv": wtkv,
        "worT": t(inputs["L0_attn_W_out_res"]),
        "wcompT": t(inputs["L0_attn_W_compress"]),
        "wqT": t(inputs["L0_attn_W_q_comp"]),
        "wkvT": t(inputs["L0_attn_W_kv_comp"]),
        "wc2lT": t(inputs["L0_attn_W_comp2lat"]),
        "gamma": c(np.asarray(inputs["L0_attn_ln_gamma"], f32).reshape(1, DC)),
        "beta": c(np.asarray(inputs["L0_attn_ln_beta"], f32).reshape(1, DC)),
        "bout": c(np.asarray(inputs["L0_attn_b_out_res"], f32).reshape(1, D)),
        "sdownb": c(np.asarray(inputs["L0_ffn_s_down_b"], f32).reshape(1, D)),
        "routerT": t(inputs["L0_ffn_router_w"]),
        "gate_w": c(np.asarray(inputs["L0_ffn_gate_w"], f32)),
        "up_w": c(np.asarray(inputs["L0_ffn_up_w"], f32)),
        "down_w": c(np.asarray(inputs["L0_ffn_down_w"], f32)),
        "sgate_w": c(np.asarray(inputs["L0_ffn_s_gate_w"], f32)),
        "sup_w": c(np.asarray(inputs["L0_ffn_s_up_w"], f32)),
        "sdown_w": c(np.asarray(inputs["L0_ffn_s_down_w"], f32)),
        "sgb2": c(np.asarray(inputs["L0_ffn_s_gate_b"], f32).reshape(22, 128).T),
        "sub2": c(np.asarray(inputs["L0_ffn_s_up_b"], f32).reshape(22, 128).T),
    }
    gw = np.asarray(inputs["L0_ffn_gate_w"], f32)
    uw = np.asarray(inputs["L0_ffn_up_w"], f32)
    shared["gupA"] = c(np.concatenate([gw[:, :, 0:768], uw[:, :, 0:768]], axis=2).reshape(E * D, 1536))
    shared["gupB"] = c(np.concatenate([gw[:, :, 768:1408], uw[:, :, 768:1408]], axis=2).reshape(E * D, 1280))
    shared["cumtri"] = np.triu(np.ones((128, 128))).astype(ml_dtypes.bfloat16)
    shared["ones128b"] = np.ones((128, 1), dtype=ml_dtypes.bfloat16)
    st24 = np.zeros((128, 1), f32)
    st24[0:24, 0] = np.arange(24, dtype=f32) * 512.0
    shared["starts24"] = st24
    kk, pp = np.meshgrid(np.arange(16, dtype=f32), np.arange(128, dtype=f32))
    shared["kp16"] = c(kk * 128 + pp)
    kk11, pp11 = np.meshgrid(np.arange(11, dtype=f32), np.arange(128, dtype=f32))
    shared["kp11"] = c(kk11 * 128 + pp11)
    dim = DC
    inv_freq = (1.0 / np.power(f32(10000.0), np.arange(0, dim, 2, dtype=f32) / f32(dim))).astype(f32)
    pos = np.arange(T, dtype=f32)
    freqs = pos[:, None] * inv_freq[None, :]
    cosf = np.cos(freqs).astype(f32)
    sinf = np.sin(freqs).astype(f32)
    cos = np.concatenate([cosf, cosf], axis=1)
    sinS = np.concatenate([-sinf, sinf], axis=1)
    scale = f32(1.0) / f32(np.sqrt(f32(dim)))
    ropetab = np.concatenate([cos * scale, sinS * scale, cos, sinS], axis=1).astype(f32)
    shared["ropetab"] = c(ropetab)
    shared["ident"] = np.eye(128, dtype=ml_dtypes.bfloat16)
    tri = np.tril(np.ones((128, 128), dtype=bool))
    shared["mask"] = np.where(tri, f32(0.0), f32(NEG)).astype(f32)
    shared["ones1"] = np.ones((1, 128), dtype=f32)
    return shared


def kernel(**inputs):
    x = np.asarray(inputs["L0_attn_x"], np.float32)
    shared = _host_prep(inputs)
    nc = build_program()
    in_maps = []
    for core in range(8):
        m = dict(shared)
        m["x"] = np.ascontiguousarray(x[core])
        in_maps.append(m)
    res = run_bass_kernel_spmd(nc, in_maps, core_ids=list(range(8)))
    out = np.stack([np.asarray(res.results[core]["out"]) for core in range(8)], axis=0)
    return out.astype(np.float32)


# version 19 (final)
# speedup vs baseline: 1.1119x; 1.1119x over previous
import numpy as np
import ml_dtypes
from contextlib import ExitStack

import concourse.bass as bass
import concourse.mybir as mybir
from concourse.bass_utils import run_bass_kernel_spmd

F32 = mybir.dt.float32
BF16 = mybir.dt.bfloat16
AF = mybir.ActivationFunctionType
ALU = mybir.AluOpType
AX = mybir.AxisListType

T = 4096
D = 2048
DL = 512
DC = 256
E = 8
I = 1408
SI = 2816
NT = T // 128
LN_EPS = 1e-5
NEG = -1e30

ENGINES = ("PE", "ACT", "DVE", "POOL", "SP")
SECTION = {"PE": "tensor", "ACT": "scalar", "DVE": "vector", "POOL": "gpsimd", "SP": "sync"}


NLANES = 32
QLANES = {"SP": 32, "POOL": 32, "ACT": 8}


class Op:
    __slots__ = ("engine", "seq", "emit", "inc", "is_dma", "needed", "cum",
                 "waits", "deps", "sem")

    def __init__(self, engine, seq, emit, is_dma):
        self.engine = engine
        self.seq = seq
        self.emit = emit
        self.is_dma = is_dma
        self.inc = 16 if is_dma else 1
        self.needed = is_dma
        self.cum = None
        self.sem = None
        self.waits = []
        self.deps = []


class Prog:

    def __init__(self, nc, block, sems):
        self.nc = nc
        self.block = block
        self.sems = sems
        self.ops = []
        self.emitted = 0
        self.last_writer = {}
        self.readers = {}
        self.totals = {s: 0 for s in sems}
        self.dma_count = {"SP": 0, "POOL": 0, "ACT": 0}
        self.barrier_vals = {}
        self.clock = {e: {} for e in ENGINES}

    def record(self, engine, emit, reads=(), writes=(), is_dma=False):
        op = Op(engine, len(self.ops), emit, is_dma)
        if is_dma:
            n = self.dma_count[engine]
            self.dma_count[engine] = n + 1
            nl = QLANES[engine]
            op.sem = f"{engine}dma{n % nl}"
            op.cum = 16 * (n // nl + 1)
        else:
            op.sem = engine
        deps = {}
        for k in reads:
            w = self.last_writer.get(k)
            if w is not None:
                deps[w.seq] = w
        for k in writes:
            w = self.last_writer.get(k)
            if w is not None:
                deps[w.seq] = w
            for r in self.readers.get(k, ()):
                deps[r.seq] = r
        op.deps = [d for s, d in sorted(deps.items()) if d is not op]
        self.ops.append(op)
        for k in reads:
            self.readers.setdefault(k, []).append(op)
        for k in writes:
            self.last_writer[k] = op
            self.readers[k] = []
        return op

    def flush(self):
        batch = self.ops[self.emitted:]
        if not batch:
            return
        start_seq = self.emitted
        for op in batch:
            for d in op.deps:
                if d.seq >= start_seq:
                    d.needed = True
        for eng in ENGINES:
            for op in reversed(batch):
                if op.engine == eng and not op.is_dma:
                    op.needed = True
                    break
        for op in self.last_writer.values():
            if op.seq >= start_seq:
                op.needed = True
        for lst in self.readers.values():
            for op in lst:
                if op.seq >= start_seq:
                    op.needed = True
        for eng in ENGINES:
            cum = self.totals.get(eng, 0)
            for op in batch:
                if op.engine != eng or op.is_dma:
                    continue
                if op.needed:
                    cum += op.inc
                op.cum = cum
            if eng in self.totals:
                self.totals[eng] = cum
        for op in batch:
            if op.is_dma:
                self.totals[op.sem] = max(self.totals[op.sem], op.cum)
        bar = dict(self.barrier_vals)
        all_bar_waits = {}
        for eng_name in ENGINES:
            bar_waits = []
            clk = self.clock[eng_name]
            for sname, v in sorted(bar.items()):
                if sname == eng_name or clk.get(sname, 0) >= v:
                    continue
                clk[sname] = v
                bar_waits.append((sname, v))
            all_bar_waits[eng_name] = bar_waits
        for op in batch:
            ws = {}
            if op.is_dma and op.cum > 16:
                ws[op.sem] = op.cum - 16
            for d in op.deps:
                if d.seq < start_seq:
                    continue
                if d.engine == op.engine == "PE":
                    continue
                assert d.needed and d.cum is not None
                if d.cum > ws.get(d.sem, 0):
                    ws[d.sem] = d.cum
            clk = self.clock[op.engine]
            for sname, v in sorted(ws.items()):
                if clk.get(sname, 0) >= v:
                    continue
                clk[sname] = v
                op.waits.append((sname, v))
        for eng_name in ENGINES:
            eng_ops = [o for o in batch if o.engine == eng_name]
            bar_waits = all_bar_waits[eng_name]
            if not eng_ops and not bar_waits:
                continue
            self._emit_section(eng_name, bar_waits, eng_ops)
        self.barrier_vals = {s: v for s, v in self.totals.items() if v > 0}
        self.emitted = len(self.ops)

    def _emit_section(self, eng_name, bar_waits, eng_ops):
        sems = self.sems

        def section(eng, _bar=bar_waits, _ops=eng_ops):
            for sname, v in _bar:
                eng.wait_ge(sems[sname], v)
            for op in _ops:
                for sname, v in op.waits:
                    eng.wait_ge(sems[sname], v)
                inst = op.emit(eng)
                if op.needed:
                    inst.then_inc(sems[op.sem], op.inc)

        getattr(self.block, SECTION[eng_name])(section)

    def finish(self):
        self.flush()
        sems = self.sems
        totals = {s: v for s, v in self.totals.items() if v > 0}

        def section(eng):
            for sname, v in sorted(totals.items()):
                eng.wait_ge(sems[sname], v)

        self.block.sync(section)


def build_program(cfg=None):
    cfg = dict(cfg or {})
    n_tiles = cfg.get("n_tiles", NT)
    moe_experts = cfg.get("moe_experts", list(range(E)))
    shared_chunks = cfg.get("shared_chunks", [(0, 6), (6, 6), (12, 6), (18, 4)])
    run_phase_b = cfg.get("phase_b", True)
    n_groups = cfg.get("n_groups", 8)
    sparse = cfg.get("sparse", True)
    NST = 24

    nc = bass.Bass("TRN2", target_bir_lowering=False)

    def din(name, shape, dt=F32):
        return nc.dram_tensor(name, list(shape), dt, kind="ExternalInput").ap()

    x_h = din("x", (T, D))
    wtkvT_h = din("wtkvT", (D, DL))
    wtkv_h = din("wtkv", (DL, D))
    worT_h = din("worT", (DL, D))
    wcompT_h = din("wcompT", (DL, DC))
    wqT_h = din("wqT", (DC, DC))
    wkvT_h = din("wkvT", (DC, 2 * DC))
    wc2lT_h = din("wc2lT", (DC, DL))
    gamma_h = din("gamma", (1, DC))
    beta_h = din("beta", (1, DC))
    bout_h = din("bout", (1, D))
    sdownb_h = din("sdownb", (1, D))
    routerT_h = din("routerT", (D, E))
    gate_h = din("gate_w", (E, D, I))
    up_h = din("up_w", (E, D, I))
    down_h = din("down_w", (E, I, D))
    sgate_h = din("sgate_w", (D, SI))
    sup_h = din("sup_w", (D, SI))
    sdown_h = din("sdown_w", (SI, D))
    sgb_h = din("sgb2", (128, 22))
    sub_h = din("sub2", (128, 22))
    rope_h = din("ropetab", (T, 4 * DC))
    ident_h = din("ident", (128, 128), BF16)
    mask_h = din("mask", (128, 128))
    ones_h = din("ones1", (1, 128))

    ln2h = nc.dram_tensor("ln2h", [16, 128, T], BF16, kind="Internal").ap()
    ln2tok = nc.dram_tensor("ln2tok", [T, D], BF16, kind="Internal").ap()
    sorted_h = nc.dram_tensor("sorted_h", [24 * 512, D], BF16, kind="Internal").ap()
    ysAB = nc.dram_tensor("ysAB", [24 * 512, 2 * D], BF16, kind="Internal").ap()
    zrows = nc.dram_tensor("zrows", [512, D], BF16, kind="Internal").ap()
    e_dram = nc.dram_tensor("e_dram", [1, 24], F32, kind="Internal").ap()
    gupA_bf = nc.dram_tensor("gupA_bf", [E * D, 1536], BF16, kind="Internal").ap()
    gupB_bf = nc.dram_tensor("gupB_bf", [E * D, 1280], BF16, kind="Internal").ap()
    downc_bf = nc.dram_tensor("downc_bf", [E * I, D], BF16, kind="Internal").ap()
    sgate_bf = nc.dram_tensor("sgate_bf", [D, SI], BF16, kind="Internal").ap()
    sup_bf = nc.dram_tensor("sup_bf", [D, SI], BF16, kind="Internal").ap()
    sdown_bf = nc.dram_tensor("sdown_bf", [SI, D], BF16, kind="Internal").ap()
    gupA_h = nc.dram_tensor("gupA", [E * D, 1536], F32, kind="ExternalInput").ap()
    gupB_h = nc.dram_tensor("gupB", [E * D, 1280], F32, kind="ExternalInput").ap()
    cumtri_h = nc.dram_tensor("cumtri", [128, 128], BF16, kind="ExternalInput").ap()
    ones128b_h = nc.dram_tensor("ones128b", [128, 1], BF16, kind="ExternalInput").ap()
    starts24_h = nc.dram_tensor("starts24", [128, 1], F32, kind="ExternalInput").ap()
    kp16_h = nc.dram_tensor("kp16", [128, 16], F32, kind="ExternalInput").ap()
    kp11_h = nc.dram_tensor("kp11", [128, 11], F32, kind="ExternalInput").ap()
    out_h = nc.dram_tensor("out", [T, D], F32, kind="ExternalOutput").ap()

    with ExitStack() as outer:
        sem_names = list(ENGINES) + [f"{q}dma{j}" for q in ("SP", "POOL", "ACT")
                                     for j in range(QLANES[q])]
        sems = {s: outer.enter_context(nc.semaphore("sem_" + s)) for s in sem_names}
        ce_sb = outer.enter_context(nc.sbuf_tensor("ce_sb", [128, NT * E], F32))
        ce12_sb = outer.enter_context(nc.sbuf_tensor("ce12_sb", [128, 2 * NT], F32))
        gr12_sb = outer.enter_context(nc.sbuf_tensor("gr12_sb", [128, 2 * NT], F32))
        eq1_sb = outer.enter_context(nc.sbuf_tensor("eq1_sb", [128, NT * E], F32))
        eq2_sb = outer.enter_context(nc.sbuf_tensor("eq2_sb", [128, NT * E], F32))
        dst_sb = outer.enter_context(nc.sbuf_tensor("dst_sb", [128, 2 * NT], mybir.dt.int32))
        block = outer.enter_context(nc.Block())
        prog = Prog(nc, block, sems)

        with ExitStack() as pa:
            def sbuf(name, shape, dt):
                return pa.enter_context(nc.sbuf_tensor(name, shape, dt))

            def psum(name, shape, dt):
                return pa.enter_context(nc.psum_tensor(name, shape, dt))

            wcomb_bf = sbuf("wcomb_bf", [128, 4 * D], BF16)
            wtkvT_bf = sbuf("wtkvT_bf", [128, 16 * DL], BF16)
            wcompT_bf = sbuf("wcompT_bf", [128, 4 * DC], BF16)
            wqT_bf = sbuf("wqT_bf", [128, 2 * DC], BF16)
            wkvT_bf = sbuf("wkvT_bf", [128, 2 * 2 * DC], BF16)
            wc2lT_bf = sbuf("wc2lT_bf", [128, 2 * DL], BF16)
            routerT_bf = sbuf("routerT_bf", [128, 16 * E], BF16)
            ident_sb = sbuf("ident_sb", [128, 128], BF16)
            mask_sb = sbuf("mask_sb", [128, 128], F32)
            gamma_bc = sbuf("gamma_bc", [128, DC], F32)
            beta_bc = sbuf("beta_bc", [128, DC], F32)
            bout_bc = sbuf("bout_bc", [128, D], F32)
            sdownb_bc = sbuf("sdownb_bc", [128, D], F32)
            ones_sb = sbuf("ones_sb", [1, 128], F32)
            pt_a = psum("pt_a", [128, 1024], BF16)
            pt_b = psum("pt_b", [128, 1024], BF16)
            pmm1 = psum("pmm1", [128, 512], F32)
            pmm2 = psum("pmm2", [128, 512], F32)
            pq = psum("pq", [128, 512], F32)
            pout2 = psum("pout2", [128, 1024], F32)
            p_o = psum("p_o", [128, 512], F32)
            cumtri_sb = sbuf("cumtri_sb", [128, 128], BF16)
            ones128b_sb = sbuf("ones128b_sb", [128, 1], BF16)
            starts24_sb = sbuf("starts24_sb", [128, 1], F32)
            sel8_bf = sbuf("sel8_bf", [128, E], BF16)
            offs_sb = sbuf("offs_sb", [1, E], F32)
            trow = sbuf("trow", [1, 32], F32)
            trow_i = sbuf("trow_i", [1, E], mybir.dt.int32)
            et8 = sbuf("et8", [128, E], F32)
            e_colv = sbuf("e_colv", [128, 1], F32)
            base_bc_sb = sbuf("base_bc_sb", [128, E], F32)
            dcol = sbuf("dcol", [128, 4], F32)

            C_SUM, C_MEAN, C_MSQ, C_VAR, C_LNV, C_RSTD, C_NEGMAX, C_SUME, C_RECIP, \
                C_M1, C_M2, C_DEN, C_RDEN, C_NM8, C_S8, C_R8 = range(16)

            def col(j):
                return cols[:, j:j + 1]

            prog.record("SP", lambda eng: eng.dma_start(out=ident_sb[:], in_=ident_h[:]),
                        writes=[("ident",)], is_dma=True)
            prog.record("SP", lambda eng: eng.dma_start(out=mask_sb[:], in_=mask_h[:]),
                        writes=[("mask",)], is_dma=True)
            prog.record("SP", lambda eng: eng.dma_start(out=ones_sb[:], in_=ones_h[:]),
                        writes=[("ones",)], is_dma=True)
            prog.record("SP", lambda eng: eng.dma_start(out=cumtri_sb[:], in_=cumtri_h[:]),
                        writes=[("cumtri",)], is_dma=True)
            prog.record("SP", lambda eng: eng.dma_start(out=ones128b_sb[:], in_=ones128b_h[:]),
                        writes=[("ones128b",)], is_dma=True)
            prog.record("SP", lambda eng: eng.dma_start(out=starts24_sb[:], in_=starts24_h[:]),
                        writes=[("starts24",)], is_dma=True)
            prog.record("DVE", lambda eng: eng.memset(offs_sb[:], 0.0),
                        writes=[("offs",)])

            def cast_load(dst_ap, src_ap, key):
                prog.record("POOL", lambda eng: eng.dma_start(out=dst_ap, in_=src_ap),
                            writes=[key], is_dma=True)

            cast_load(wtkvT_bf[:].rearrange("p (k n) -> p k n", k=16),
                      wtkvT_h.rearrange("(k p) n -> p k n", p=128), ("wtkvT",))
            cast_load(wcompT_bf[:].rearrange("p (k n) -> p k n", k=4),
                      wcompT_h.rearrange("(k p) n -> p k n", p=128), ("wcompT",))
            cast_load(wqT_bf[:].rearrange("p (k n) -> p k n", k=2),
                      wqT_h.rearrange("(k p) n -> p k n", p=128), ("wqT",))
            cast_load(wkvT_bf[:].rearrange("p (k n) -> p k n", k=2),
                      wkvT_h.rearrange("(k p) n -> p k n", p=128), ("wkvT",))
            cast_load(wc2lT_bf[:].rearrange("p (k n) -> p k n", k=2),
                      wc2lT_h.rearrange("(k p) n -> p k n", p=128), ("wc2lT",))
            cast_load(routerT_bf[:].rearrange("p (k n) -> p k n", k=16),
                      routerT_h.rearrange("(k p) n -> p k n", p=128), ("routerT",))

            with ExitStack() as prep:
                wtkv_f = prep.enter_context(nc.sbuf_tensor("wtkv_f", [128, 4 * D], F32))
                worT_f = prep.enter_context(nc.sbuf_tensor("worT_f", [128, 4 * D], F32))
                prog.record("SP", lambda eng: eng.dma_start(
                    out=wtkv_f[:].rearrange("p (k n) -> p k n", k=4),
                    in_=wtkv_h.rearrange("(k p) n -> p k n", p=128)),
                    writes=[("wtkv_f",)], is_dma=True)
                prog.record("SP", lambda eng: eng.dma_start(
                    out=worT_f[:].rearrange("p (k n) -> p k n", k=4),
                    in_=worT_h.rearrange("(k p) n -> p k n", p=128)),
                    writes=[("worT_f",)], is_dma=True)
                prog.record("DVE", lambda eng: eng.tensor_tensor(
                    out=wcomb_bf[:], in0=wtkv_f[:], in1=worT_f[:], op=ALU.add),
                    reads=[("wtkv_f",), ("worT_f",)], writes=[("wcomb",)])

                with ExitStack() as prep2:
                    rows = prep2.enter_context(nc.sbuf_tensor("rows", [1, 2 * D + 2 * DC], F32))
                    prog.record("SP", lambda eng: eng.dma_start(out=rows[0:1, 0:D], in_=bout_h[:]),
                                writes=[("rows",)], is_dma=True)
                    prog.record("SP", lambda eng: eng.dma_start(out=rows[0:1, D:2 * D], in_=sdownb_h[:]),
                                writes=[("rows",)], is_dma=True)
                    prog.record("SP", lambda eng: eng.dma_start(out=rows[0:1, 2 * D:2 * D + DC], in_=gamma_h[:]),
                                writes=[("rows",)], is_dma=True)
                    prog.record("SP", lambda eng: eng.dma_start(out=rows[0:1, 2 * D + DC:2 * D + 2 * DC], in_=beta_h[:]),
                                writes=[("rows",)], is_dma=True)

                    def bc_row(dst, src_off, width, tag):
                        nchunk = (width + 511) // 512
                        for s in range(nchunk):
                            w = min(512, width - s * 512)
                            prog.record("PE", lambda eng, s=s, w=w: eng.matmul(
                                pout2[:, s % 2 * 512:s % 2 * 512 + w],
                                ones_sb[0:1, :],
                                rows[0:1, src_off + s * 512:src_off + s * 512 + w],
                                start=True, stop=True),
                                reads=[("ones",), ("rows",)], writes=[("pbc", s % 2)])
                            prog.record("DVE", lambda eng, s=s, w=w: eng.tensor_copy(
                                out=dst[:, s * 512:s * 512 + w],
                                in_=pout2[:, s % 2 * 512:s % 2 * 512 + w]),
                                reads=[("pbc", s % 2)], writes=[(tag,)])

                    bc_row(bout_bc, 0, D, "bout_bc")
                    bc_row(sdownb_bc, D, D, "sdownb_bc")
                    bc_row(gamma_bc, 2 * D, DC, "gamma_bc")
                    bc_row(beta_bc, 2 * D + DC, DC, "beta_bc")
                    prog.flush()
            prog.flush()

            if sparse:
                prog.record("DVE", lambda eng: eng.memset(exp_bf[:], 0.0),
                            writes=[("exp", "src")])
                for zq in range(2):
                    prog.record("POOL", lambda eng, zq=zq: eng.dma_start(
                        out=zrows[zq * 256:(zq + 1) * 256, :].rearrange("(q p) c -> p q c", p=128),
                        in_=exp_bf[:].rearrange("p (q c) -> p q c", q=2)),
                        reads=[("exp", "src")], writes=[("zrows",)], is_dma=True)
                for sp_i in range(24):
                    prog.record("POOL", lambda eng, sp_i=sp_i: eng.dma_start(
                        out=sorted_h[sp_i * 512:(sp_i + 1) * 512, :], in_=zrows[:]),
                        reads=[("zrows",)], writes=[("sorted",)], is_dma=True)
                down_srcv = down_h.rearrange("e i d -> (e i) d")
                for ch in range(8):
                    r0, r1 = ch * 2048, (ch + 1) * 2048
                    prog.record("POOL", lambda eng, r0=r0, r1=r1: eng.dma_start(
                        out=gupA_bf[r0:r1, :], in_=gupA_h[r0:r1, :]),
                        writes=[("wbf",)], is_dma=True)
                    prog.record("POOL", lambda eng, r0=r0, r1=r1: eng.dma_start(
                        out=gupB_bf[r0:r1, :], in_=gupB_h[r0:r1, :]),
                        writes=[("wbf",)], is_dma=True)
                    d0, d1 = ch * 1408, (ch + 1) * 1408
                    prog.record("POOL", lambda eng, d0=d0, d1=d1, down_srcv=down_srcv: eng.dma_start(
                        out=downc_bf[d0:d1, :], in_=down_srcv[d0:d1, :]),
                        writes=[("wbf",)], is_dma=True)
                    s0, s1 = ch * 256, (ch + 1) * 256
                    prog.record("POOL", lambda eng, s0=s0, s1=s1: eng.dma_start(
                        out=sgate_bf[s0:s1, :], in_=sgate_h[s0:s1, :]),
                        writes=[("wbf",)], is_dma=True)
                    prog.record("POOL", lambda eng, s0=s0, s1=s1: eng.dma_start(
                        out=sup_bf[s0:s1, :], in_=sup_h[s0:s1, :]),
                        writes=[("wbf",)], is_dma=True)
                    z0, z1 = ch * 352, (ch + 1) * 352
                    prog.record("POOL", lambda eng, z0=z0, z1=z1: eng.dma_start(
                        out=sdown_bf[z0:z1, :], in_=sdown_h[z0:z1, :]),
                        writes=[("wbf",)], is_dma=True)

            kT0 = sbuf("kT0", [128, T], BF16)
            kT1 = sbuf("kT1", [128, T], BF16)
            v_sb = sbuf("v_sb", [128, NT * DC], BF16)
            stage = sbuf("stage", [128, 16 * 512], BF16)
            x_sb = [sbuf(f"x_sb{p}", [128, D], F32) for p in range(2)]
            rope_sb = [sbuf(f"rope_sb{p}", [128, 4 * DC], F32) for p in range(2)]
            xn_bf = sbuf("xn_bf", [128, D], BF16)
            xnT = sbuf("xnT", [128, D], BF16)
            kvlat_bf = sbuf("kvlat_bf", [128, DL], BF16)
            kvlatT = sbuf("kvlatT", [128, DL], BF16)
            c_sb = sbuf("c_sb", [128, DC], F32)
            t1_sb = sbuf("t1_sb", [128, DC], F32)
            t2_sb = sbuf("t2_sb", [128, DC], F32)
            ch_bf = sbuf("ch_bf", [128, DC], BF16)
            chT = sbuf("chT", [128, DC], BF16)
            rtmp = c_sb
            rcos = t1_sb
            qr_bf = sbuf("qr_bf", [128, DC], BF16)
            kr_bf = sbuf("kr_bf", [128, DC], BF16)
            qrTb = [sbuf(f"qrT{p}", [128, DC], BF16) for p in range(2)]
            scores = sbuf("scores", [128, T], F32)
            exp_bf = sbuf("exp_bf", [128, T], BF16)
            expT = sbuf("expT", [128, T], BF16)
            o_bf = sbuf("o_bf", [128, DC], BF16)
            oT = sbuf("oT", [128, DC], BF16)
            z_bf = sbuf("z_bf", [128, DL], BF16)
            zT = sbuf("zT", [128, DL], BF16)
            xb_sb = sbuf("xb_sb", [128, D], F32)
            r1_sb = sbuf("r1_sb", [128, D], F32)
            ln2_bf = sbuf("ln2_bf", [128, D], BF16)
            cols = sbuf("cols", [128, 16], F32)
            colsB_sb = sbuf("colsB_sb", [128, 16], F32)
            r8a = sbuf("r8a", [128, E], F32)
            r8b = sbuf("r8b", [128, E], F32)
            r8c = sbuf("r8c", [128, E], F32)

            def transpose_burst(rec, src_bf, n_blocks, dst, dst_off_blocks, tag, pt, pkey,
                                dkey=None):
                done = 0
                dk = dkey if dkey is not None else (tag, "dst")
                while done < n_blocks:
                    burst = min(8, n_blocks - done)
                    for b in range(burst):
                        rec("PE", lambda eng, b=b, done=done: eng.transpose(
                            pt[:, b * 128:(b + 1) * 128],
                            src_bf[:, (done + b) * 128:(done + b + 1) * 128],
                            ident_sb[:]),
                            reads=[(tag, "src"), ("ident",)], writes=[pkey])
                    o0 = (dst_off_blocks + done) * 128
                    rec("DVE", lambda eng, burst=burst, o0=o0: eng.tensor_copy(
                        out=dst[:, o0:o0 + burst * 128], in_=pt[:, 0:burst * 128]),
                        reads=[pkey], writes=[dk])
                    done += burst

            def ln_cols(rec, src_key, src_ap, width, dump_ap, dump_key, C, ck, c2k):
                rec("DVE", lambda eng: eng.reduce_sum(
                    out=C(C_SUM), in_=src_ap, axis=AX.X),
                    reads=[src_key], writes=[ck])
                rec("ACT", lambda eng: eng.activation(
                    out=dump_ap, in_=src_ap, func=AF.Square, bias=0.0, scale=1.0,
                    accum_out=C(C_MSQ)),
                    reads=[src_key], writes=[dump_key, c2k])
                rec("DVE", lambda eng: eng.tensor_scalar(
                    out=C(C_MEAN), in0=C(C_SUM), scalar1=1.0 / width, scalar2=None,
                    op0=ALU.mult), reads=[ck], writes=[ck])
                rec("DVE", lambda eng: eng.tensor_tensor(
                    out=C(C_VAR), in0=C(C_MEAN), in1=C(C_MEAN), op=ALU.mult),
                    reads=[ck], writes=[ck])
                rec("DVE", lambda eng: eng.tensor_scalar(
                    out=C(C_VAR), in0=C(C_MSQ), scalar1=1.0 / width,
                    scalar2=C(C_VAR), op0=ALU.mult, op1=ALU.subtract),
                    reads=[ck, c2k], writes=[ck])
                rec("DVE", lambda eng: eng.tensor_scalar(
                    out=C(C_VAR), in0=C(C_VAR), scalar1=LN_EPS, scalar2=None,
                    op0=ALU.add), reads=[ck], writes=[ck])
                rec("ACT", lambda eng: eng.activation(
                    out=C(C_LNV), in_=C(C_VAR), func=AF.Ln, bias=0.0, scale=1.0),
                    reads=[ck], writes=[ck])
                rec("ACT", lambda eng: eng.activation(
                    out=C(C_RSTD), in_=C(C_LNV), func=AF.Exp, bias=0.0, scale=-0.5),
                    reads=[ck], writes=[ck])

            def colB(j):
                return colsB_sb[:, j:j + 1]

            def issue_tile_loads(i, rec):
                par = i % 2
                rec("ACT", lambda eng, i=i, par=par: eng.dma_start(
                    out=x_sb[par][:], in_=x_h[i * 128:(i + 1) * 128, :]),
                    writes=[("x", par)], is_dma=True)
                rec("ACT", lambda eng, i=i, par=par: eng.dma_start(
                    out=rope_sb[par][:], in_=rope_h[i * 128:(i + 1) * 128, :]),
                    writes=[("rope", par)], is_dma=True)

            def front_tile(i, rec):
                par = i % 2
                issue_tile_loads(i, rec)
                x_t = x_sb[par]
                rope_t = rope_sb[par]
                xkey, rkey = ("x", par), ("rope", par)

                ln_cols(rec, xkey, x_t[:], D, xn_bf[:], ("xn", "src"), col, ("cols",), ("cols2",))
                rec("DVE", lambda eng, x_t=x_t: eng.tensor_scalar(
                    out=xn_bf[:], in0=x_t[:], scalar1=col(C_MEAN), scalar2=col(C_RSTD),
                    op0=ALU.subtract, op1=ALU.mult),
                    reads=[xkey, ("cols",)], writes=[("xn", "src")])

                transpose_burst(rec, xn_bf, 16, xnT, 0, "xn", pt_a, ("pt", 0))

                for k in range(16):
                    rec("PE", lambda eng, k=k: eng.matmul(
                        pmm1[:, 0:DL], xnT[:, k * 128:(k + 1) * 128],
                        wtkvT_bf[:, k * DL:(k + 1) * DL],
                        start=(k == 0), stop=(k == 15)),
                        reads=[("xn", "dst"), ("wtkvT",)], writes=[("pmm1",)])
                rec("ACT", lambda eng: eng.copy(out=kvlat_bf[:], in_=pmm1[:, 0:DL]),
                            reads=[("pmm1",)], writes=[("kvlat", "src")])
                transpose_burst(rec, kvlat_bf, 4, kvlatT, 0, "kvlat", pt_a, ("pt", 0))

                for k in range(4):
                    rec("PE", lambda eng, k=k: eng.matmul(
                        pq[:, 0:DC], kvlatT[:, k * 128:(k + 1) * 128],
                        wcompT_bf[:, k * DC:(k + 1) * DC],
                        start=(k == 0), stop=(k == 3)),
                        reads=[("kvlat", "dst"), ("wcompT",)], writes=[("pq",)])
                rec("DVE", lambda eng: eng.tensor_copy(out=c_sb[:], in_=pq[:, 0:DC]),
                            reads=[("pq",)], writes=[("c_sb",)])

                ln_cols(rec, ("c_sb",), c_sb[:], DC, ch_bf[:], ("ch", "src"), col, ("cols",), ("cols2",))
                rec("DVE", lambda eng: eng.tensor_scalar(
                    out=t1_sb[:], in0=c_sb[:], scalar1=col(C_MEAN), scalar2=col(C_RSTD),
                    op0=ALU.subtract, op1=ALU.mult),
                    reads=[("c_sb",), ("cols",)], writes=[("t1",)])
                rec("DVE", lambda eng: eng.tensor_tensor(
                    out=t2_sb[:], in0=t1_sb[:], in1=gamma_bc[:], op=ALU.mult),
                    reads=[("t1",), ("gamma_bc",)], writes=[("t2",)])
                rec("DVE", lambda eng: eng.tensor_tensor(
                    out=ch_bf[:], in0=t2_sb[:], in1=beta_bc[:], op=ALU.add),
                    reads=[("t2",), ("beta_bc",)], writes=[("ch", "src")])
                transpose_burst(rec, ch_bf, 2, chT, 0, "ch", pt_a, ("pt", 0))

                for k in range(2):
                    rec("PE", lambda eng, k=k: eng.matmul(
                        pq[:, 0:DC], chT[:, k * 128:(k + 1) * 128],
                        wqT_bf[:, k * DC:(k + 1) * DC],
                        start=(k == 0), stop=(k == 1)),
                        reads=[("ch", "dst"), ("wqT",)], writes=[("pq",)])
                for k in range(2):
                    rec("PE", lambda eng, k=k: eng.matmul(
                        pmm1[:, 0:2 * DC], chT[:, k * 128:(k + 1) * 128],
                        wkvT_bf[:, k * 2 * DC:(k + 1) * 2 * DC],
                        start=(k == 0), stop=(k == 1)),
                        reads=[("ch", "dst"), ("wkvT",)], writes=[("pmm1",)])

                def rope_apply(psrc, pkey, cos_off, sin_off, dst_bf, dkey, rt):
                    rec("DVE", lambda eng: eng.tensor_tensor(
                        out=rtmp[:, 0:128], in0=psrc[:, 128:256],
                        in1=rt[:, sin_off:sin_off + 128], op=ALU.mult),
                        reads=[pkey, rkey], writes=[("c_sb",)])
                    rec("DVE", lambda eng: eng.tensor_tensor(
                        out=rtmp[:, 128:256], in0=psrc[:, 0:128],
                        in1=rt[:, sin_off + 128:sin_off + 256], op=ALU.mult),
                        reads=[pkey, rkey], writes=[("c_sb",)])
                    rec("DVE", lambda eng: eng.tensor_tensor(
                        out=rcos[:], in0=psrc[:, 0:256],
                        in1=rt[:, cos_off:cos_off + 256], op=ALU.mult),
                        reads=[pkey, rkey], writes=[("t1",)])
                    rec("DVE", lambda eng: eng.tensor_tensor(
                        out=dst_bf[:], in0=rcos[:], in1=rtmp[:], op=ALU.add),
                        reads=[("t1",), ("c_sb",)], writes=[dkey])

                rope_apply(pq, ("pq",), 0, DC, qr_bf, ("qr", "src"), rope_t)
                rope_apply(pmm1, ("pmm1",), 2 * DC, 3 * DC, kr_bf, ("kr", "src"), rope_t)
                rec("ACT", lambda eng, i=i: eng.copy(
                    out=v_sb[:, i * DC:(i + 1) * DC], in_=pmm1[:, DC:2 * DC]),
                    reads=[("pmm1",)], writes=[("v", i)])

                transpose_burst(rec, qr_bf, 2, qrTb[par], 0, "qr", pt_a, ("pt", 0),
                                dkey=("qrT", par))
                for kc in range(2):
                    rec("PE", lambda eng, kc=kc: eng.transpose(
                        pt_a[:, kc * 128:(kc + 1) * 128],
                        kr_bf[:, kc * 128:(kc + 1) * 128], ident_sb[:]),
                        reads=[("kr", "src"), ("ident",)], writes=[("pt", 0)])
                for kc, kT in ((0, kT0), (1, kT1)):
                    rec("DVE", lambda eng, kc=kc, kT=kT, i=i: eng.tensor_copy(
                        out=kT[:, i * 128:(i + 1) * 128], in_=pt_a[:, kc * 128:(kc + 1) * 128]),
                        reads=[("pt", 0)], writes=[("kT", kc, i)])

            def back_tile(i, rec):
                par = i % 2
                x_t = x_sb[par]
                xkey = ("x", par)
                qrT = qrTb[par]

                w_keys = (i + 1) * 128
                nch = (w_keys + 511) // 512
                for j in range(nch):
                    ncols = min(512, w_keys - j * 512)
                    ps = (pmm2, p_o)[j % 2]
                    pskey = ("pmm2",) if j % 2 == 0 else ("p_o",)
                    kreads = [("kT", kc, t) for kc in range(2)
                              for t in range(j * 4, min((j * 512 + ncols + 127) // 128, i + 1))]
                    for kc, kT in ((0, kT0), (1, kT1)):
                        rec("PE", lambda eng, j=j, ncols=ncols, kc=kc, kT=kT, ps=ps, qrT=qrT: eng.matmul(
                            ps[:, 0:ncols], qrT[:, kc * 128:(kc + 1) * 128],
                            kT[:, j * 512:j * 512 + ncols],
                            start=(kc == 0), stop=(kc == 1)),
                            reads=[("qrT", par)] + kreads, writes=[pskey])
                    if j == nch - 1:
                        if ncols > 128:
                            rec("ACT", lambda eng, j=j, ncols=ncols, ps=ps: eng.copy(
                                out=scores[:, j * 512:j * 512 + ncols - 128],
                                in_=ps[:, 0:ncols - 128]),
                                reads=[pskey], writes=[("scores",)])
                        rec("DVE", lambda eng, j=j, ncols=ncols, ps=ps: eng.scalar_tensor_tensor(
                            out=scores[:, j * 512 + ncols - 128:j * 512 + ncols],
                            in0=ps[:, ncols - 128:ncols], scalar=0.0, in1=mask_sb[:],
                            op0=ALU.add, op1=ALU.add),
                            reads=[pskey, ("mask",)], writes=[("scores",)])
                    else:
                        if j % 2 == 0:
                            rec("DVE", lambda eng, j=j, ncols=ncols, ps=ps: eng.tensor_copy(
                                out=scores[:, j * 512:j * 512 + ncols], in_=ps[:, 0:ncols]),
                                reads=[pskey], writes=[("scores",)])
                        else:
                            rec("ACT", lambda eng, j=j, ncols=ncols, ps=ps: eng.copy(
                                out=scores[:, j * 512:j * 512 + ncols], in_=ps[:, 0:ncols]),
                                reads=[pskey], writes=[("scores",)])

                rec("DVE", lambda eng, w_keys=w_keys: eng.reduce_max(
                    out=colB(C_NEGMAX), in_=scores[:, 0:w_keys], axis=AX.X, negate=True),
                    reads=[("scores",)], writes=[("colsB",)])
                rec("ACT", lambda eng, w_keys=w_keys: eng.activation(
                    out=exp_bf[:, 0:w_keys], in_=scores[:, 0:w_keys], func=AF.Exp,
                    bias=colB(C_NEGMAX), scale=1.0, accum_out=colB(C_SUME)),
                    reads=[("scores",), ("colsB",)], writes=[("exp", "src"), ("colsB2",)])
                rec("DVE", lambda eng: eng.reciprocal(
                    out=colB(C_RECIP), in_=colB(C_SUME)),
                    reads=[("colsB2",)], writes=[("colsB",)])

                transpose_burst(rec, exp_bf, i + 1, expT, 0, "exp", pt_b, ("pt", 1))
                for b in range(i + 1):
                    rec("PE", lambda eng, b=b, i=i: eng.matmul(
                        pmm2[:, 0:DC], expT[:, b * 128:(b + 1) * 128],
                        v_sb[:, b * DC:(b + 1) * DC],
                        start=(b == 0), stop=(b == i)),
                        reads=[("exp", "dst"), ("v", b)], writes=[("pmm2",)])
                rec("DVE", lambda eng: eng.tensor_scalar(
                    out=o_bf[:], in0=pmm2[:, 0:DC], scalar1=colB(C_RECIP), scalar2=None,
                    op0=ALU.mult), reads=[("pmm2",), ("colsB",)], writes=[("o", "src")])

                transpose_burst(rec, o_bf, 2, oT, 0, "o", pt_b, ("pt", 1))
                for k in range(2):
                    rec("PE", lambda eng, k=k: eng.matmul(
                        p_o[:, 0:DL], oT[:, k * 128:(k + 1) * 128],
                        wc2lT_bf[:, k * DL:(k + 1) * DL],
                        start=(k == 0), stop=(k == 1)),
                        reads=[("o", "dst"), ("wc2lT",)], writes=[("p_o",)])
                rec("ACT", lambda eng: eng.copy(out=z_bf[:], in_=p_o[:, 0:DL]),
                            reads=[("p_o",)], writes=[("z", "src")])
                transpose_burst(rec, z_bf, 4, zT, 0, "z", pt_b, ("pt", 1))

                rec("DVE", lambda eng, x_t=x_t: eng.tensor_tensor(
                    out=xb_sb[:], in0=x_t[:], in1=bout_bc[:], op=ALU.add),
                    reads=[xkey, ("bout_bc",)], writes=[("xb",)])
                for h in range(2):
                    for s in range(2):
                        for k in range(4):
                            rec("PE", lambda eng, h=h, s=s, k=k: eng.matmul(
                                pout2[:, s * 512:(s + 1) * 512],
                                zT[:, k * 128:(k + 1) * 128],
                                wcomb_bf[:, k * D + h * 1024 + s * 512:k * D + h * 1024 + (s + 1) * 512],
                                start=(k == 0), stop=(k == 3)),
                                reads=[("z", "dst"), ("wcomb",)], writes=[("pout2",)])
                    rec("DVE", lambda eng, h=h: eng.tensor_tensor(
                        out=r1_sb[:, h * 1024:(h + 1) * 1024], in0=pout2[:],
                        in1=xb_sb[:, h * 1024:(h + 1) * 1024], op=ALU.add),
                        reads=[("pout2",), ("xb",)], writes=[("r1",)])

                rec("DVE", lambda eng: eng.tensor_tensor(
                    out=xb_sb[:], in0=r1_sb[:], in1=sdownb_bc[:], op=ALU.add),
                    reads=[("r1",), ("sdownb_bc",)], writes=[("xb",)])
                rec("SP", lambda eng, i=i: eng.dma_start(
                    out=out_h[i * 128:(i + 1) * 128, :], in_=xb_sb[:]),
                    reads=[("xb",)], writes=[("out", i)], is_dma=True)

                ln_cols(rec, ("r1",), r1_sb[:], D, exp_bf[:, 0:D], ("exp", "src"),
                        colB, ("colsB",), ("colsB2",))
                rec("DVE", lambda eng: eng.tensor_scalar(
                    out=ln2_bf[:], in0=r1_sb[:], scalar1=colB(C_MEAN), scalar2=colB(C_RSTD),
                    op0=ALU.subtract, op1=ALU.mult),
                    reads=[("r1",), ("colsB",)], writes=[("ln2", "src")])
                rec("SP", lambda eng, i=i: eng.dma_start(
                    out=ln2tok[i * 128:(i + 1) * 128, :], in_=ln2_bf[:]),
                    reads=[("ln2", "src")], writes=[("ln2tok", i)], is_dma=True)

                g4 = i % 4
                done = 0
                while done < 16:
                    for b in range(8):
                        rec("PE", lambda eng, b=b, done=done: eng.transpose(
                            pt_b[:, b * 128:(b + 1) * 128],
                            ln2_bf[:, (done + b) * 128:(done + b + 1) * 128],
                            ident_sb[:]),
                            reads=[("ln2", "src"), ("ident",)], writes=[("pt", 1)])
                    rec("DVE", lambda eng, done=done, g4=g4: eng.tensor_copy(
                        out=stage[:].rearrange("p (c t) -> p c t", c=16)[:, done:done + 8, g4 * 128:(g4 + 1) * 128],
                        in_=pt_b[:].rearrange("p (c t) -> p c t", c=8)),
                        reads=[("pt", 1)], writes=[("stage",)])
                    done += 8

                for k in range(16):
                    rec("PE", lambda eng, k=k, g4=g4: eng.matmul(
                        pout2[:, 0:E], stage[:, k * 512 + g4 * 128:k * 512 + (g4 + 1) * 128],
                        routerT_bf[:, k * E:(k + 1) * E],
                        start=(k == 0), stop=(k == 15)),
                        reads=[("stage",), ("routerT",)], writes=[("pout2",)])
                rec("DVE", lambda eng: eng.tensor_copy(out=r8a[:], in_=pout2[:, 0:E]),
                            reads=[("pout2",)], writes=[("r8a",)])
                rec("DVE", lambda eng: eng.reduce_max(
                    out=colB(C_NM8), in_=r8a[:], axis=AX.X, negate=True),
                    reads=[("r8a",)], writes=[("colsB",)])
                rec("ACT", lambda eng: eng.activation(
                    out=r8b[:], in_=r8a[:], func=AF.Exp, bias=colB(C_NM8), scale=1.0,
                    accum_out=colB(C_S8)),
                    reads=[("r8a",), ("colsB",)], writes=[("r8b",), ("colsB2",)])
                rec("DVE", lambda eng: eng.reciprocal(out=colB(C_R8), in_=colB(C_S8)),
                            reads=[("colsB2",)], writes=[("colsB",)])
                rec("DVE", lambda eng: eng.tensor_scalar(
                    out=r8a[:], in0=r8b[:], scalar1=colB(C_R8), scalar2=None, op0=ALU.mult),
                    reads=[("r8b",), ("colsB",)], writes=[("r8a",)])
                rec("DVE", lambda eng: eng.reduce_max(
                    out=colB(C_M1), in_=r8a[:], axis=AX.X),
                    reads=[("r8a",)], writes=[("colsB",)])
                rec("DVE", lambda eng, i=i: eng.tensor_scalar(
                    out=eq1_sb[:, i * E:(i + 1) * E], in0=r8a[:], scalar1=colB(C_M1),
                    scalar2=None, op0=ALU.is_equal),
                    reads=[("r8a",), ("colsB",)], writes=[("eq1", i)])
                rec("DVE", lambda eng, i=i: eng.tensor_copy(
                    out=r8b[:], in_=eq1_sb[:, i * E:(i + 1) * E]),
                    reads=[("eq1", i)], writes=[("r8b",)])
                rec("DVE", lambda eng: eng.tensor_scalar(
                    out=r8c[:], in0=r8b[:], scalar1=-1.0, scalar2=1.0, op0=ALU.mult, op1=ALU.add),
                    reads=[("r8b",)], writes=[("r8c",)])
                rec("DVE", lambda eng: eng.tensor_tensor(
                    out=r8c[:], in0=r8a[:], in1=r8c[:], op=ALU.mult),
                    reads=[("r8a",), ("r8c",)], writes=[("r8c",)])
                rec("DVE", lambda eng: eng.reduce_max(
                    out=colB(C_M2), in_=r8c[:], axis=AX.X),
                    reads=[("r8c",)], writes=[("colsB",)])
                rec("DVE", lambda eng, i=i: eng.tensor_scalar(
                    out=eq2_sb[:, i * E:(i + 1) * E], in0=r8c[:], scalar1=colB(C_M2),
                    scalar2=None, op0=ALU.is_equal),
                    reads=[("r8c",), ("colsB",)], writes=[("eq2", i)])
                rec("DVE", lambda eng, i=i: eng.tensor_copy(
                    out=r8c[:], in_=eq2_sb[:, i * E:(i + 1) * E]),
                    reads=[("eq2", i)], writes=[("r8c",)])
                rec("DVE", lambda eng: eng.tensor_tensor(
                    out=r8b[:], in0=r8b[:], in1=r8c[:], op=ALU.add),
                    reads=[("r8b",), ("r8c",)], writes=[("r8b",)])
                rec("DVE", lambda eng: eng.tensor_tensor(
                    out=r8b[:], in0=r8a[:], in1=r8b[:], op=ALU.mult),
                    reads=[("r8a",), ("r8b",)], writes=[("r8b",)])
                rec("DVE", lambda eng: eng.tensor_tensor(
                    out=colB(C_DEN), in0=colB(C_M1), in1=colB(C_M2), op=ALU.add),
                    reads=[("colsB",)], writes=[("colsB",)])
                rec("DVE", lambda eng: eng.reciprocal(out=colB(C_RDEN), in_=colB(C_DEN)),
                            reads=[("colsB",)], writes=[("colsB",)])
                rec("DVE", lambda eng, i=i: eng.tensor_scalar(
                    out=ce_sb[:, i * E:(i + 1) * E], in0=r8b[:], scalar1=colB(C_RDEN),
                    scalar2=None, op0=ALU.mult),
                    reads=[("r8b",), ("colsB",)], writes=[("ce", i)])
                rec("DVE", lambda eng, i=i: eng.tensor_tensor(
                    out=r8b[:], in0=eq1_sb[:, i * E:(i + 1) * E],
                    in1=ce_sb[:, i * E:(i + 1) * E], op=ALU.mult),
                    reads=[("eq1", i), ("ce", i)], writes=[("r8b",)])
                rec("DVE", lambda eng, i=i: eng.reduce_sum(
                    out=ce12_sb[:, 2 * i:2 * i + 1], in_=r8b[:], axis=AX.X),
                    reads=[("r8b",)], writes=[("ce12", i)])
                rec("DVE", lambda eng, i=i: eng.tensor_tensor(
                    out=r8b[:], in0=eq2_sb[:, i * E:(i + 1) * E],
                    in1=ce_sb[:, i * E:(i + 1) * E], op=ALU.mult),
                    reads=[("eq2", i), ("ce", i)], writes=[("r8b",)])
                rec("DVE", lambda eng, i=i: eng.reduce_sum(
                    out=ce12_sb[:, 2 * i + 1:2 * i + 2], in_=r8b[:], axis=AX.X),
                    reads=[("r8b",)], writes=[("ce12", i)])
                rec("DVE", lambda eng, i=i: eng.tensor_tensor(
                    out=sel8_bf[:], in0=eq1_sb[:, i * E:(i + 1) * E],
                    in1=eq2_sb[:, i * E:(i + 1) * E], op=ALU.add),
                    reads=[("eq1", i), ("eq2", i)], writes=[("sel8",)])
                rec("PE", lambda eng: eng.matmul(
                    pout2[:, 0:E], cumtri_sb[:], sel8_bf[:], start=True, stop=True),
                    reads=[("sel8",), ("cumtri",)], writes=[("pout2",)])
                rec("PE", lambda eng: eng.matmul(
                    pout2[0:1, E:2 * E], ones128b_sb[:], sel8_bf[:], start=True, stop=True),
                    reads=[("sel8",), ("ones128b",)], writes=[("pout2",)])
                rec("PE", lambda eng: eng.matmul(
                    pout2[:, 2 * E:3 * E], ones_sb[0:1, :], offs_sb[:], start=True, stop=True),
                    reads=[("offs",), ("ones",)], writes=[("pout2",)])
                rec("DVE", lambda eng: eng.tensor_copy(
                    out=et8[:], in_=pout2[:, 2 * E:3 * E]),
                    reads=[("pout2",)], writes=[("et8",)])
                rec("DVE", lambda eng: eng.tensor_tensor(
                    out=r8c[:], in0=pout2[:, 0:E], in1=et8[:], op=ALU.add),
                    reads=[("pout2",), ("et8",)], writes=[("r8c",)])
                rec("DVE", lambda eng, i=i: eng.tensor_tensor(
                    out=r8b[:], in0=eq1_sb[:, i * E:(i + 1) * E], in1=r8c[:], op=ALU.mult),
                    reads=[("eq1", i), ("r8c",)], writes=[("r8b",)])
                rec("DVE", lambda eng, i=i: eng.reduce_sum(
                    out=gr12_sb[:, 2 * i:2 * i + 1], in_=r8b[:], axis=AX.X),
                    reads=[("r8b",)], writes=[("gr12", i)])
                rec("DVE", lambda eng, i=i: eng.tensor_tensor(
                    out=r8b[:], in0=eq2_sb[:, i * E:(i + 1) * E], in1=r8c[:], op=ALU.mult),
                    reads=[("eq2", i), ("r8c",)], writes=[("r8b",)])
                rec("DVE", lambda eng, i=i: eng.reduce_sum(
                    out=gr12_sb[:, 2 * i + 1:2 * i + 2], in_=r8b[:], axis=AX.X),
                    reads=[("r8b",)], writes=[("gr12", i)])
                rec("DVE", lambda eng: eng.tensor_tensor(
                    out=offs_sb[:], in0=offs_sb[:], in1=pout2[0:1, E:2 * E], op=ALU.add),
                    reads=[("offs",), ("pout2",)], writes=[("offs",)])

                if g4 == 3:
                    rec("SP", lambda eng, i=i: eng.dma_start(
                        out=ln2h[:, :, (i - 3) * 128:(i + 1) * 128].rearrange("c p t -> p c t"),
                        in_=stage[:].rearrange("p (c t) -> p c t", c=16)),
                        reads=[("stage",)], writes=[("ln2h", i // 4)], is_dma=True)

            def collect(fn, i):
                L = []
                fn(i, lambda *a, **k: L.append((a, k)))
                return L

            def merge_emit(A, B):
                na, nb = len(A), len(B)
                ia = ib = 0
                while ia < na or ib < nb:
                    if ib >= nb or (ia < na and ia * nb <= ib * na):
                        a, k = A[ia]; ia += 1
                    else:
                        a, k = B[ib]; ib += 1
                    prog.record(*a, **k)

            for a, k in collect(front_tile, 0):
                prog.record(*a, **k)
            for i in range(n_tiles):
                A = collect(front_tile, i + 1) if i + 1 < n_tiles else []
                merge_emit(A, collect(back_tile, i))

            if sparse:
                prog.record("DVE", lambda eng: eng.memset(trow[0:1, 8:16], 0.0),
                            writes=[("trow",)])
                for j_ in range(8):
                    prog.record("DVE", lambda eng, j_=j_: eng.tensor_scalar(
                        out=trow[0:1, 24:32], in0=offs_sb[:], scalar1=float(512 * j_),
                        scalar2=None, op0=ALU.subtract),
                        reads=[("offs",)], writes=[("trow",)])
                    prog.record("ACT", lambda eng: eng.activation(
                        out=trow[0:1, 24:32], in_=trow[0:1, 24:32], func=AF.Sign,
                        bias=0.0, scale=1.0),
                        reads=[("trow",)], writes=[("trow",)])
                    prog.record("DVE", lambda eng: eng.tensor_scalar(
                        out=trow[0:1, 24:32], in0=trow[0:1, 24:32], scalar1=0.0,
                        scalar2=None, op0=ALU.max),
                        reads=[("trow",)], writes=[("trow",)])
                    prog.record("DVE", lambda eng: eng.tensor_tensor(
                        out=trow[0:1, 8:16], in0=trow[0:1, 8:16], in1=trow[0:1, 24:32],
                        op=ALU.add),
                        reads=[("trow",)], writes=[("trow",)])
                prog.record("DVE", lambda eng: eng.tensor_scalar(
                    out=trow[0:1, 16:24], in0=trow[0:1, 8:16], scalar1=512.0,
                    scalar2=None, op0=ALU.mult),
                    reads=[("trow",)], writes=[("trow",)])
                prog.record("DVE", lambda eng: eng.memset(trow[0:1, 0:8], 0.0),
                            writes=[("trow",)])
                for e_ in range(1, E):
                    prog.record("DVE", lambda eng, e_=e_: eng.tensor_tensor(
                        out=trow[0:1, e_:e_ + 1], in0=trow[0:1, e_ - 1:e_],
                        in1=trow[0:1, 16 + e_ - 1:16 + e_], op=ALU.add),
                        reads=[("trow",)], writes=[("trow",)])
                prog.record("PE", lambda eng: eng.matmul(
                    pout2[:, 24:32], ones_sb[0:1, :], trow[0:1, 0:8], start=True, stop=True),
                    reads=[("trow",), ("ones",)], writes=[("pout2",)])
                prog.record("DVE", lambda eng: eng.tensor_copy(
                    out=base_bc_sb[:], in_=pout2[:, 24:32]),
                    reads=[("pout2",)], writes=[("base_bc",)])
                prog.record("PE", lambda eng: eng.matmul(
                    pout2[0:24, 24:32], ones_sb[0:1, 0:24], trow[0:1, 0:8], start=True, stop=True),
                    reads=[("trow",), ("ones",)], writes=[("pout2",)])
                prog.record("DVE", lambda eng: eng.tensor_scalar(
                    out=et8[0:24, :], in0=pout2[0:24, 24:32],
                    scalar1=starts24_sb[0:24, 0:1], scalar2=None, op0=ALU.subtract),
                    reads=[("pout2",), ("starts24",)], writes=[("et8",)])
                prog.record("ACT", lambda eng: eng.activation(
                    out=et8[0:24, :], in_=et8[0:24, :], func=AF.Sign, bias=0.0, scale=1.0),
                    reads=[("et8",)], writes=[("et8",)])
                prog.record("DVE", lambda eng: eng.tensor_scalar(
                    out=et8[0:24, :], in0=et8[0:24, :], scalar1=0.0, scalar2=None,
                    op0=ALU.max),
                    reads=[("et8",)], writes=[("et8",)])
                prog.record("DVE", lambda eng: eng.tensor_scalar(
                    out=et8[0:24, :], in0=et8[0:24, :], scalar1=-1.0, scalar2=1.0,
                    op0=ALU.mult, op1=ALU.add),
                    reads=[("et8",)], writes=[("et8",)])
                prog.record("DVE", lambda eng: eng.reduce_sum(
                    out=e_colv[0:24, 0:1], in_=et8[0:24, :], axis=AX.X),
                    reads=[("et8",)], writes=[("e_colv",)])
                prog.record("DVE", lambda eng: eng.tensor_scalar(
                    out=e_colv[0:24, 0:1], in0=e_colv[0:24, 0:1], scalar1=-1.0,
                    scalar2=None, op0=ALU.add),
                    reads=[("e_colv",)], writes=[("e_colv",)])
                prog.record("SP", lambda eng: eng.dma_start(
                    out=e_dram[0:1, :].rearrange("r c -> c r"), in_=e_colv[0:24, 0:1]),
                    reads=[("e_colv",)], writes=[("e_dram",)], is_dma=True)
                for i in range(n_tiles):
                    for which in (0, 1):
                        eqs = eq1_sb if which == 0 else eq2_sb
                        eqk = ("eq1", i) if which == 0 else ("eq2", i)
                        prog.record("DVE", lambda eng, i=i, eqs=eqs: eng.tensor_tensor(
                            out=r8b[:], in0=eqs[:, i * E:(i + 1) * E], in1=base_bc_sb[:],
                            op=ALU.mult),
                            reads=[eqk, ("base_bc",)], writes=[("r8b",)])
                        prog.record("DVE", lambda eng, which=which: eng.reduce_sum(
                            out=dcol[:, which:which + 1], in_=r8b[:], axis=AX.X),
                            reads=[("r8b",)], writes=[("dcol",)])
                        prog.record("DVE", lambda eng, i=i, which=which: eng.tensor_scalar(
                            out=dcol[:, 2 + which:3 + which], in0=dcol[:, which:which + 1],
                            scalar1=gr12_sb[:, 2 * i + which:2 * i + which + 1],
                            scalar2=-1.0, op0=ALU.add, op1=ALU.add),
                            reads=[("dcol",), ("gr12", i)], writes=[("dcol",)])
                        prog.record("DVE", lambda eng, i=i, which=which: eng.tensor_copy(
                            out=dst_sb[:, 2 * i + which:2 * i + which + 1],
                            in_=dcol[:, 2 + which:3 + which]),
                            reads=[("dcol",)], writes=[("dst", i)])

            prog.flush()

        if run_phase_b:
            with ExitStack() as pb:
                def sbuf(name, shape, dt):
                    return pb.enter_context(nc.sbuf_tensor(name, shape, dt))

                if sparse:
                    gup_sb = [sbuf(f"gup_sb{p}", [128, 16 * 1536], BF16) for p in range(2)]
                    gate_sb = [gup_sb[0][:, 0:16 * 768], gup_sb[1][:, 0:16 * 768]]
                    upw_sb = [gup_sb[0][:, 16 * 768:32 * 768], gup_sb[1][:, 16 * 768:32 * 768]]
                else:
                    gate_sb = [sbuf(f"gate_sb{p}", [128, 16 * 768], BF16) for p in range(2)]
                    upw_sb = [sbuf(f"upw_sb{p}", [128, 16 * 768], BF16) for p in range(2)]
                down_sb = [sbuf(f"down_sb{p}", [128, 6 * D], BF16) for p in range(2)]
                ln2g = [sbuf(f"ln2g{p}", [128, 16 * 512], BF16) for p in range(2)]
                act_bf = sbuf("act_bf", [128, 6 * 512], BF16)
                y_sb = [sbuf(f"y_sb{p}", [128, D], F32) for p in range(2)]
                sgb_sb = sbuf("sgb_sb", [128, 22], F32)
                sub_sb = sbuf("sub_sb", [128, 22], F32)
                if sparse:
                    sort_stg = [sbuf("sort_stg", [128, D], BF16)]
                pg = [pb.enter_context(nc.psum_tensor(f"pg{p}", [128, 512], F32)) for p in range(2)]
                pu = [pb.enter_context(nc.psum_tensor("pu0", [128, 512], F32))] * 2
                pyA = pb.enter_context(nc.psum_tensor("pyA", [128, 1024], F32))
                pyB = pb.enter_context(nc.psum_tensor("pyB", [128, 1024], F32))

                from concourse.bass_core_rust import IndirectOffsetOnAxis
                sort_ctr = [0]

                def emit_sort_tile():
                    if not sparse or sort_ctr[0] >= n_tiles:
                        return
                    i = sort_ctr[0]
                    sort_ctr[0] += 1
                    spar = 0
                    stg = sort_stg[spar]
                    skey = ("sstg", spar)
                    prog.record("SP", lambda eng, i=i, stg=stg: eng.dma_start(
                        out=stg[:], in_=ln2tok[i * 128:(i + 1) * 128, :]),
                        reads=[("ln2tok", i)], writes=[skey], is_dma=True)
                    for which in (0, 1):
                        prog.record("POOL", lambda eng, i=i, which=which, stg=stg: eng.indirect_dma_start(
                            out=sorted_h[:],
                            out_offset=IndirectOffsetOnAxis(dst_sb[:, 2 * i + which:2 * i + which + 1], 0),
                            in_=stg[:], in_offset=None),
                            reads=[skey, ("dst", i)], writes=[("sorted",)], is_dma=True)

                prog.record("SP", lambda eng: eng.dma_start(out=sgb_sb[:], in_=sgb_h[:]),
                            writes=[("sgb",)], is_dma=True)
                prog.record("SP", lambda eng: eng.dma_start(out=sub_sb[:], in_=sub_h[:]),
                            writes=[("sub",)], is_dma=True)

                passes = []
                if not sparse:
                    for e in moe_experts:
                        passes.append(("moe", e, 0, 6))
                        passes.append(("moe", e, 6, 5))
                for c0, cn in shared_chunks:
                    passes.append(("sh", None, c0, cn))

                def load_pass_weights(p_idx):
                    kind, e, ic0, icn = passes[p_idx]
                    par = p_idx % 2
                    W = icn * 128
                    g_dst = gate_sb[par].rearrange("p (k w) -> p k w", k=16)[:, :, 0:W]
                    u_dst = upw_sb[par].rearrange("p (k w) -> p k w", k=16)[:, :, 0:W]
                    d_dst = down_sb[par][:].rearrange("p (c h) -> p c h", c=6)[:, 0:icn, :]
                    if kind == "moe":
                        g_src = gate_h[e, :, ic0 * 128:(ic0 + icn) * 128].rearrange("(k p) w -> p k w", p=128)
                        u_src = up_h[e, :, ic0 * 128:(ic0 + icn) * 128].rearrange("(k p) w -> p k w", p=128)
                        d_src = down_h[e, ic0 * 128:(ic0 + icn) * 128, :].rearrange("(c p) h -> p c h", p=128)
                    else:
                        gsh = sgate_bf if sparse else sgate_h
                        ush = sup_bf if sparse else sup_h
                        dsh = sdown_bf if sparse else sdown_h
                        g_src = gsh[:, ic0 * 128:(ic0 + icn) * 128].rearrange("(k p) w -> p k w", p=128)
                        u_src = ush[:, ic0 * 128:(ic0 + icn) * 128].rearrange("(k p) w -> p k w", p=128)
                        d_src = dsh[ic0 * 128:(ic0 + icn) * 128, :].rearrange("(c p) h -> p c h", p=128)
                    wq = "SP" if (kind == "sh" and sparse) else "POOL"
                    prog.record(wq, lambda eng: eng.dma_start(out=g_dst, in_=g_src),
                                writes=[("gup", par, k_) for k_ in range(8)], is_dma=True)
                    prog.record(wq, lambda eng: eng.dma_start(out=u_dst, in_=u_src),
                                writes=[("gup", par, k_) for k_ in range(8, 16)], is_dma=True)
                    prog.record(wq, lambda eng: eng.dma_start(out=d_dst, in_=d_src),
                                writes=[("dnc", par, k_) for k_ in range(6)], is_dma=True)

                def load_group(p_idx, g):
                    gpar = g % 2
                    prog.record("SP", lambda eng, g=g, gpar=gpar: eng.dma_start(
                        out=ln2g[gpar][:].rearrange("p (c t) -> p c t", c=16),
                        in_=ln2h[:, :, g * 512:(g + 1) * 512].rearrange("c p t -> p c t")),
                        reads=[("ln2h", g)], writes=[("ln2g", gpar)], is_dma=True)

                if passes:
                    load_pass_weights(0)
                for p_idx, (kind, e, ic0, icn) in enumerate(passes):
                    par = p_idx % 2
                    if p_idx + 1 < len(passes):
                        load_pass_weights(p_idx + 1)
                    W = icn * 128
                    load_group(p_idx, 0)
                    for g in range(n_groups):
                        gpar = g % 2
                        if g + 1 < n_groups:
                            load_group(p_idx, g + 1)
                        for c in range(icn):
                            cp = c % 2
                            for k in range(16):
                                prog.record("PE", lambda eng, c=c, k=k, cp=cp, par=par, gpar=gpar: eng.matmul(
                                    pg[cp][:, 0:512],
                                    gate_sb[par][:, k * 768 + c * 128:k * 768 + (c + 1) * 128],
                                    ln2g[gpar][:, k * 512:(k + 1) * 512],
                                    start=(k == 0), stop=(k == 15)),
                                    reads=[("gup", par, 0), ("gup", par, 7), ("ln2g", gpar)], writes=[("pg", cp)])
                            for k in range(16):
                                prog.record("PE", lambda eng, c=c, k=k, cp=cp, par=par, gpar=gpar: eng.matmul(
                                    pu[cp][:, 0:512],
                                    upw_sb[par][:, k * 768 + c * 128:k * 768 + (c + 1) * 128],
                                    ln2g[gpar][:, k * 512:(k + 1) * 512],
                                    start=(k == 0), stop=(k == 15)),
                                    reads=[("gup", par, 8), ("gup", par, 15), ("ln2g", gpar)], writes=[("pu",)])
                            acs = act_bf[:, c * 512:(c + 1) * 512]
                            if kind == "sh":
                                gc = ic0 + c
                                prog.record("ACT", lambda eng, cp=cp, gc=gc, acs=acs: eng.activation(
                                    out=acs, in_=pg[cp][:, 0:512], func=AF.Silu,
                                    bias=sgb_sb[:, gc:gc + 1], scale=1.0),
                                    reads=[("pg", cp), ("sgb",)], writes=[("act", c)])
                                prog.record("DVE", lambda eng, c=c, cp=cp, gc=gc, acs=acs: eng.scalar_tensor_tensor(
                                    out=acs, in0=pu[cp][:, 0:512],
                                    scalar=sub_sb[:, gc:gc + 1], in1=acs,
                                    op0=ALU.add, op1=ALU.mult),
                                    reads=[("pu",), ("act", c), ("sub",)], writes=[("act", c)])
                            else:
                                prog.record("ACT", lambda eng, cp=cp, acs=acs: eng.activation(
                                    out=acs, in_=pg[cp][:, 0:512], func=AF.Silu,
                                    bias=0.0, scale=1.0),
                                    reads=[("pg", cp)], writes=[("act", c)])
                                prog.record("DVE", lambda eng, c=c, cp=cp, acs=acs: eng.scalar_tensor_tensor(
                                    out=acs, in0=pu[cp][:, 0:512],
                                    scalar=0.0, in1=acs,
                                    op0=ALU.add, op1=ALU.mult),
                                    reads=[("pu",), ("act", c)], writes=[("act", c)])
                        for t in range(4):
                            gt = g * 4 + t
                            for c in range(icn):
                                for hc in range(4):
                                    pyT = pyA if hc < 2 else pyB
                                    pykey = ("pyA",) if hc < 2 else ("pyB",)
                                    prog.record("PE", lambda eng, t=t, c=c, hc=hc, icn=icn, par=par, pyT=pyT: eng.matmul(
                                        pyT[:, (hc % 2) * 512:(hc % 2 + 1) * 512],
                                        act_bf[:, c * 512 + t * 128:c * 512 + (t + 1) * 128],
                                        down_sb[par][:, c * D + hc * 512:c * D + (hc + 1) * 512],
                                        start=(c == 0), stop=(c == icn - 1)),
                                        reads=[("act", c), ("dnc", par, c)], writes=[pykey])
                            ypar = t % 2
                            if kind == "moe":
                                prog.record("DVE", lambda eng, gt=gt, e=e, ypar=ypar: eng.tensor_scalar(
                                    out=y_sb[ypar][:, 0:1024], in0=pyA[:],
                                    scalar1=ce_sb[:, gt * E + e:gt * E + e + 1],
                                    scalar2=None, op0=ALU.mult),
                                    reads=[("pyA",), ("ce", gt)], writes=[("y", ypar, 0)])
                                prog.record("ACT", lambda eng, gt=gt, e=e, ypar=ypar: eng.activation(
                                    out=y_sb[ypar][:, 1024:2048], in_=pyB[:], func=AF.Identity,
                                    bias=0.0, scale=ce_sb[:, gt * E + e:gt * E + e + 1]),
                                    reads=[("pyB",), ("ce", gt)], writes=[("y", ypar, 1)])
                            else:
                                prog.record("DVE", lambda eng, ypar=ypar: eng.tensor_copy(
                                    out=y_sb[ypar][:, 0:1024], in_=pyA[:]),
                                    reads=[("pyA",)], writes=[("y", ypar, 0)])
                                prog.record("ACT", lambda eng, ypar=ypar: eng.copy(
                                    out=y_sb[ypar][:, 1024:2048], in_=pyB[:]),
                                    reads=[("pyB",)], writes=[("y", ypar, 1)])
                            prog.record("POOL", lambda eng, gt=gt, ypar=ypar: eng.dma_start(
                                out=out_h[gt * 128:(gt + 1) * 128, :], in_=y_sb[ypar][:],
                                accum_op=ALU.add),
                                reads=[("y", ypar, 0), ("y", ypar, 1), ("out", gt)],
                                writes=[("out", gt)], is_dma=True)
                        emit_sort_tile()

                while sparse and sort_ctr[0] < n_tiles:
                    emit_sort_tile()

                if sparse:
                    from concourse.bass_core_rust import IndirectOffsetOnAxis
                    prog.flush()
                    ident2_sb = sbuf("ident2_sb", [128, 128], BF16)
                    ones_b_sb = sbuf("ones_b_sb", [1, 128], F32)
                    prog.record("SP", lambda eng: eng.dma_start(out=ident2_sb[:], in_=ident_h[:]),
                                writes=[("ident2",)], is_dma=True)
                    prog.record("SP", lambda eng: eng.dma_start(out=ones_b_sb[:], in_=ones_h[:]),
                                writes=[("ones_b",)], is_dma=True)
                    kp16_sb = sbuf("kp16_sb", [128, 16], F32)
                    kp11_sb = sbuf("kp11_sb", [128, 11], F32)
                    idxf = sbuf("idxf", [128, 27], F32)
                    idx_gu = [sbuf(f"idx_gu{p}", [128, 16], mybir.dt.int32) for p in range(2)]
                    idx_dn = [sbuf(f"idx_dn{p}", [128, 11], mybir.dt.int32) for p in range(2)]
                    ecol2 = sbuf("ecol2", [128, 2], F32)
                    e_bc_all = sbuf("e_bc_all", [128, 24], F32)
                    e_row_sb = sbuf("e_row_sb", [1, 24], F32)
                    lnTs = [ln2g[0], ln2g[1]]
                    down_view = downc_bf

                    prog.record("SP", lambda eng: eng.dma_start(out=kp16_sb[:], in_=kp16_h[:]),
                                writes=[("kp16",)], is_dma=True)
                    prog.record("SP", lambda eng: eng.dma_start(out=kp11_sb[:], in_=kp11_h[:]),
                                writes=[("kp11",)], is_dma=True)
                    prog.record("SP", lambda eng: eng.dma_start(out=e_row_sb[:], in_=e_dram[:]),
                                writes=[("e_row",)], is_dma=True)
                    prog.record("PE", lambda eng: eng.matmul(
                        pg[0][:, 0:24], ones_b_sb[0:1, :], e_row_sb[:], start=True, stop=True),
                        reads=[("e_row",), ("ones_b",)], writes=[("pg", 0)])
                    prog.record("DVE", lambda eng: eng.tensor_copy(
                        out=e_bc_all[:], in_=pg[0][:, 0:24]),
                        reads=[("pg", 0)], writes=[("e_bc",)])

                    def sp_gather(p_idx):
                        s_, h_ = divmod(p_idx, 2)
                        par = p_idx % 2
                        icn = 6 if h_ == 0 else 5
                        W = icn * 128
                        gsrc = gupA_bf if h_ == 0 else gupB_bf
                        W2 = 1536 if h_ == 0 else 1280
                        k0 = 0 if h_ == 0 else 6
                        prog.record("DVE", lambda eng, s_=s_: eng.tensor_scalar(
                            out=ecol2[:, 0:1], in0=e_bc_all[:, s_:s_ + 1], scalar1=2048.0,
                            scalar2=None, op0=ALU.mult),
                            reads=[("e_bc",)], writes=[("ecol2",)])
                        prog.record("DVE", lambda eng, s_=s_: eng.tensor_scalar(
                            out=ecol2[:, 1:2], in0=e_bc_all[:, s_:s_ + 1], scalar1=1408.0,
                            scalar2=None, op0=ALU.mult),
                            reads=[("e_bc",)], writes=[("ecol2",)])
                        prog.record("DVE", lambda eng: eng.tensor_scalar(
                            out=idxf[:, 0:16], in0=kp16_sb[:], scalar1=ecol2[:, 0:1],
                            scalar2=None, op0=ALU.add),
                            reads=[("kp16",), ("ecol2",)], writes=[("idxf",)])
                        prog.record("DVE", lambda eng: eng.tensor_scalar(
                            out=idxf[:, 16:27], in0=kp11_sb[:], scalar1=ecol2[:, 1:2],
                            scalar2=None, op0=ALU.add),
                            reads=[("kp11",), ("ecol2",)], writes=[("idxf",)])
                        prog.record("DVE", lambda eng, par=par: eng.tensor_copy(
                            out=idx_gu[par][:], in_=idxf[:, 0:16]),
                            reads=[("idxf",)], writes=[("idxgu", par)])
                        prog.record("DVE", lambda eng, par=par: eng.tensor_copy(
                            out=idx_dn[par][:], in_=idxf[:, 16:27]),
                            reads=[("idxf",)], writes=[("idxdn", par)])
                        for k in range(16):
                            prog.record("POOL", lambda eng, k=k, par=par, W2=W2, gsrc=gsrc: eng.indirect_dma_start(
                                out=gup_sb[par][:, k * 1536:k * 1536 + W2], out_offset=None,
                                in_=gsrc,
                                in_offset=IndirectOffsetOnAxis(idx_gu[par][:, k:k + 1], 0)),
                                reads=[("idxgu", par)], writes=[("gup", par, k)], is_dma=True)
                        for k in range(icn):
                            prog.record("POOL", lambda eng, k=k, k0=k0, par=par: eng.indirect_dma_start(
                                out=down_sb[par][:, k * D:(k + 1) * D], out_offset=None,
                                in_=down_view,
                                in_offset=IndirectOffsetOnAxis(idx_dn[par][:, k0 + k:k0 + k + 1], 0)),
                                reads=[("idxdn", par)], writes=[("dnc", par, k)], is_dma=True)

                    def sp_compute_half(s_, h_):
                        par = (s_ * 2 + h_) % 2
                        icn = 6 if h_ == 0 else 5
                        upoff = 768 if h_ == 0 else 640
                        lnT = lnTs[s_ % 2]
                        for c in range(icn):
                            cp = c % 2
                            for k in range(16):
                                prog.record("PE", lambda eng, c=c, k=k, cp=cp, par=par: eng.matmul(
                                    pg[cp][:, 0:512],
                                    gup_sb[par][:, k * 1536 + c * 128:k * 1536 + (c + 1) * 128],
                                    lnT[:, k * 512:(k + 1) * 512],
                                    start=(k == 0), stop=(k == 15)),
                                    reads=[("gup", par, k), ("lnT", s_ % 2, k)], writes=[("pg", cp)])
                            for k in range(16):
                                prog.record("PE", lambda eng, c=c, k=k, par=par, upoff=upoff: eng.matmul(
                                    pu[0][:, 0:512],
                                    gup_sb[par][:, k * 1536 + upoff + c * 128:k * 1536 + upoff + (c + 1) * 128],
                                    lnT[:, k * 512:(k + 1) * 512],
                                    start=(k == 0), stop=(k == 15)),
                                    reads=[("gup", par, k), ("lnT", s_ % 2, k)], writes=[("pu",)])
                            acs = act_bf[:, c * 512:(c + 1) * 512]
                            prog.record("ACT", lambda eng, cp=cp, acs=acs: eng.activation(
                                out=acs, in_=pg[cp][:, 0:512], func=AF.Silu,
                                bias=0.0, scale=1.0),
                                reads=[("pg", cp)], writes=[("act", c)])
                            prog.record("DVE", lambda eng, c=c, acs=acs: eng.scalar_tensor_tensor(
                                out=acs, in0=pu[0][:, 0:512],
                                scalar=0.0, in1=acs,
                                op0=ALU.add, op1=ALU.mult),
                                reads=[("pu",), ("act", c)], writes=[("act", c)])
                        for t in range(4):
                            for c in range(icn):
                                for hc in range(4):
                                    pyT = pyA if hc < 2 else pyB
                                    pykey = ("pyA",) if hc < 2 else ("pyB",)
                                    prog.record("PE", lambda eng, t=t, c=c, hc=hc, icn=icn, par=par, pyT=pyT: eng.matmul(
                                        pyT[:, (hc % 2) * 512:(hc % 2 + 1) * 512],
                                        act_bf[:, c * 512 + t * 128:c * 512 + (t + 1) * 128],
                                        down_sb[par][:, c * D + hc * 512:c * D + (hc + 1) * 512],
                                        start=(c == 0), stop=(c == icn - 1)),
                                        reads=[("act", c), ("dnc", par, c)], writes=[pykey])
                            ypar = t % 2
                            yv = y_sb[ypar][:, 0:1024].bitcast(BF16)
                            prog.record("DVE", lambda eng, yv=yv: eng.tensor_copy(
                                out=yv[:, 0:1024], in_=pyA[:]),
                                reads=[("pyA",)], writes=[("y", ypar, 0)])
                            prog.record("ACT", lambda eng, yv=yv: eng.copy(
                                out=yv[:, 1024:2048], in_=pyB[:]),
                                reads=[("pyB",)], writes=[("y", ypar, 1)])
                            r0 = s_ * 512 + t * 128
                            c0 = 0 if h_ == 0 else D
                            prog.record("SP", lambda eng, r0=r0, c0=c0, yv=yv: eng.dma_start(
                                out=ysAB[r0:r0 + 128, c0:c0 + D], in_=yv[:]),
                                reads=[("y", ypar, 0), ("y", ypar, 1)],
                                writes=[("ys", s_, t, h_)], is_dma=True)

                    def sp_lnT(s_):
                        buf = lnTs[s_ % 2]
                        for kc in range(16):
                            prog.record("SP", lambda eng, s_=s_, kc=kc, buf=buf: eng.dma_start(
                                out=buf[:, kc * 512:(kc + 1) * 512],
                                in_=sorted_h[s_ * 512:(s_ + 1) * 512, kc * 128:(kc + 1) * 128],
                                transpose=True),
                                reads=[("sorted",)], writes=[("lnT", s_ % 2, kc)], is_dma=True)

                    n_sp = 2 * NST
                    if n_sp:
                        sp_gather(0)
                        if n_sp > 1:
                            sp_gather(1)
                        sp_lnT(0)
                    for p_idx in range(n_sp):
                        s_, h_ = divmod(p_idx, 2)
                        if h_ == 0 and s_ + 1 < NST:
                            sp_lnT(s_ + 1)
                        sp_compute_half(s_, h_)
                        if p_idx + 2 < n_sp:
                            sp_gather(p_idx + 2)

                    prog.flush()
                    g1 = [gup_sb[p][:, 0:4096] for p in range(2)]
                    g2 = [gup_sb[p][:, 4096:8192] for p in range(2)]
                    yc = [gup_sb[p][:, 8192:12288].bitcast(F32) for p in range(2)]
                    oacc = [down_sb[p][:, 0:4096].bitcast(F32) for p in range(2)]
                    for i in range(n_tiles):
                        par = i % 2
                        prog.record("POOL", lambda eng, i=i, par=par: eng.indirect_dma_start(
                            out=g1[par], out_offset=None, in_=ysAB[:],
                            in_offset=IndirectOffsetOnAxis(dst_sb[:, 2 * i:2 * i + 1], 0)),
                            reads=[("dstf", i)], writes=[("g1", par)], is_dma=True)
                        prog.record("POOL", lambda eng, i=i, par=par: eng.indirect_dma_start(
                            out=g2[par], out_offset=None, in_=ysAB[:],
                            in_offset=IndirectOffsetOnAxis(dst_sb[:, 2 * i + 1:2 * i + 2], 0)),
                            reads=[("dstf", i)], writes=[("g2", par)], is_dma=True)
                        prog.record("DVE", lambda eng, i=i, par=par: eng.tensor_scalar(
                            out=yc[par], in0=g1[par][:, 0:D], scalar1=ce12_sb[:, 2 * i:2 * i + 1],
                            scalar2=None, op0=ALU.mult),
                            reads=[("g1", par), ("ce12", i)], writes=[("yc", par)])
                        prog.record("DVE", lambda eng, i=i, par=par: eng.scalar_tensor_tensor(
                            out=yc[par], in0=g1[par][:, D:2 * D], scalar=ce12_sb[:, 2 * i:2 * i + 1],
                            in1=yc[par], op0=ALU.mult, op1=ALU.add),
                            reads=[("g1", par), ("yc", par), ("ce12", i)], writes=[("yc", par)])
                        prog.record("DVE", lambda eng, i=i, par=par: eng.scalar_tensor_tensor(
                            out=yc[par], in0=g2[par][:, 0:D], scalar=ce12_sb[:, 2 * i + 1:2 * i + 2],
                            in1=yc[par], op0=ALU.mult, op1=ALU.add),
                            reads=[("g2", par), ("yc", par), ("ce12", i)], writes=[("yc", par)])
                        prog.record("DVE", lambda eng, i=i, par=par: eng.scalar_tensor_tensor(
                            out=yc[par], in0=g2[par][:, D:2 * D], scalar=ce12_sb[:, 2 * i + 1:2 * i + 2],
                            in1=yc[par], op0=ALU.mult, op1=ALU.add),
                            reads=[("g2", par), ("yc", par), ("ce12", i)], writes=[("yc", par)])
                        prog.record("SP", lambda eng, i=i, par=par: eng.dma_start(
                            out=oacc[par], in_=out_h[i * 128:(i + 1) * 128, :]),
                            reads=[("out", i)], writes=[("oacc", par)], is_dma=True)
                        prog.record("DVE", lambda eng, i=i, par=par: eng.tensor_tensor(
                            out=yc[par], in0=yc[par], in1=oacc[par], op=ALU.add),
                            reads=[("yc", par), ("oacc", par)], writes=[("yc", par)])
                        prog.record("SP", lambda eng, i=i, par=par: eng.dma_start(
                            out=out_h[i * 128:(i + 1) * 128, :], in_=yc[par]),
                            reads=[("yc", par), ("out", i)], writes=[("out", i)], is_dma=True)
                prog.flush()

        prog.finish()
    return nc


def _host_prep(inputs):
    f32 = np.float32
    c = np.ascontiguousarray

    def t(a):
        return c(np.asarray(a, dtype=f32).T)

    wtkv = c(np.asarray(inputs["L0_attn_W_to_kv"], f32))
    shared = {
        "wtkvT": t(wtkv),
        "wtkv": wtkv,
        "worT": t(inputs["L0_attn_W_out_res"]),
        "wcompT": t(inputs["L0_attn_W_compress"]),
        "wqT": t(inputs["L0_attn_W_q_comp"]),
        "wkvT": t(inputs["L0_attn_W_kv_comp"]),
        "wc2lT": t(inputs["L0_attn_W_comp2lat"]),
        "gamma": c(np.asarray(inputs["L0_attn_ln_gamma"], f32).reshape(1, DC)),
        "beta": c(np.asarray(inputs["L0_attn_ln_beta"], f32).reshape(1, DC)),
        "bout": c(np.asarray(inputs["L0_attn_b_out_res"], f32).reshape(1, D)),
        "sdownb": c(np.asarray(inputs["L0_ffn_s_down_b"], f32).reshape(1, D)),
        "routerT": t(inputs["L0_ffn_router_w"]),
        "gate_w": c(np.asarray(inputs["L0_ffn_gate_w"], f32)),
        "up_w": c(np.asarray(inputs["L0_ffn_up_w"], f32)),
        "down_w": c(np.asarray(inputs["L0_ffn_down_w"], f32)),
        "sgate_w": c(np.asarray(inputs["L0_ffn_s_gate_w"], f32)),
        "sup_w": c(np.asarray(inputs["L0_ffn_s_up_w"], f32)),
        "sdown_w": c(np.asarray(inputs["L0_ffn_s_down_w"], f32)),
        "sgb2": c(np.asarray(inputs["L0_ffn_s_gate_b"], f32).reshape(22, 128).T),
        "sub2": c(np.asarray(inputs["L0_ffn_s_up_b"], f32).reshape(22, 128).T),
    }
    gw = np.asarray(inputs["L0_ffn_gate_w"], f32)
    uw = np.asarray(inputs["L0_ffn_up_w"], f32)
    shared["gupA"] = c(np.concatenate([gw[:, :, 0:768], uw[:, :, 0:768]], axis=2).reshape(E * D, 1536))
    shared["gupB"] = c(np.concatenate([gw[:, :, 768:1408], uw[:, :, 768:1408]], axis=2).reshape(E * D, 1280))
    shared["cumtri"] = np.triu(np.ones((128, 128))).astype(ml_dtypes.bfloat16)
    shared["ones128b"] = np.ones((128, 1), dtype=ml_dtypes.bfloat16)
    st24 = np.zeros((128, 1), f32)
    st24[0:24, 0] = np.arange(24, dtype=f32) * 512.0
    shared["starts24"] = st24
    kk, pp = np.meshgrid(np.arange(16, dtype=f32), np.arange(128, dtype=f32))
    shared["kp16"] = c(kk * 128 + pp)
    kk11, pp11 = np.meshgrid(np.arange(11, dtype=f32), np.arange(128, dtype=f32))
    shared["kp11"] = c(kk11 * 128 + pp11)
    dim = DC
    inv_freq = (1.0 / np.power(f32(10000.0), np.arange(0, dim, 2, dtype=f32) / f32(dim))).astype(f32)
    pos = np.arange(T, dtype=f32)
    freqs = pos[:, None] * inv_freq[None, :]
    cosf = np.cos(freqs).astype(f32)
    sinf = np.sin(freqs).astype(f32)
    cos = np.concatenate([cosf, cosf], axis=1)
    sinS = np.concatenate([-sinf, sinf], axis=1)
    scale = f32(1.0) / f32(np.sqrt(f32(dim)))
    ropetab = np.concatenate([cos * scale, sinS * scale, cos, sinS], axis=1).astype(f32)
    shared["ropetab"] = c(ropetab)
    shared["ident"] = np.eye(128, dtype=ml_dtypes.bfloat16)
    tri = np.tril(np.ones((128, 128), dtype=bool))
    shared["mask"] = np.where(tri, f32(0.0), f32(NEG)).astype(f32)
    shared["ones1"] = np.ones((1, 128), dtype=f32)
    return shared


def kernel(**inputs):
    x = np.asarray(inputs["L0_attn_x"], np.float32)
    shared = _host_prep(inputs)
    nc = build_program()
    in_maps = []
    for core in range(8):
        m = dict(shared)
        m["x"] = np.ascontiguousarray(x[core])
        in_maps.append(m)
    res = run_bass_kernel_spmd(nc, in_maps, core_ids=list(range(8)))
    out = np.stack([np.asarray(res.results[core]["out"]) for core in range(8)], axis=0)
    return out.astype(np.float32)
